# version 07
# baseline (speedup 1.0000x reference)
import os
import numpy as np
from contextlib import ExitStack

import concourse.bass as bass
import concourse.mybir as mybir
from concourse.bass_utils import run_bass_kernel_spmd

F32 = mybir.dt.float32
F16 = mybir.dt.float16
ALU = mybir.AluOpType
AF = mybir.ActivationFunctionType
AX = mybir.AxisListType

NCORE = 8
BL, L, D, NDT = 2, 2048, 1024, 8
NST, ND, NE, DFF = 16, 64, 4, 4096
CH = 256
NCH = L // CH
TC = 512
NTC = L // TC
NFS = 4
NKF = 8
NFB = NFS * NKF
EPS = 1e-5


class Ctr:
    def __init__(self, es, nc, name, dma=False):
        self.h = es.enter_context(nc.semaphore(name))
        self.n = 0
        self.name = name
        self.dma = dma


SCHED = {}


class Emit:
    ENGS = ("sync", "gpsimd", "tensor", "vector", "scalar")

    def __init__(self):
        self.ops = {e: [] for e in self.ENGS}
        self.last_wait = {e: {} for e in self.ENGS}
        SCHED.clear()
        SCHED.update({e: [] for e in self.ENGS})

    def wait(self, eng, ev):
        if ev is None:
            return
        ctr, val = ev
        if ctr.dma:
            val = ctr.n
        if val is None or val <= 0:
            return
        if self.last_wait[eng].get(ctr.name, 0) >= val:
            return
        self.last_wait[eng][ctr.name] = val
        SCHED[eng].append(("w", ctr.name, val))
        self.ops[eng].append(lambda e, c=ctr, v=val: e.wait_ge(c.h, v))

    def do(self, eng, fn, inc=None, k=1):
        if eng in ("vector", "scalar", "gpsimd"):
            self.ops[eng].append(lambda e: e.drain())
        if inc is None:
            SCHED[eng].append(("n", None, 0))
            self.ops[eng].append(fn)
            return None

        def f(e, fn=fn, c=inc, k=k):
            fn(e).then_inc(c.h, k)
        SCHED[eng].append(("i", inc.name, k))
        self.ops[eng].append(f)
        inc.n += k
        return (inc, inc.n)


def build_program(stage="full"):
    nc = bass.Bass("TRN2", target_bir_lowering=False)

    def din(name, shape, dt=F32):
        return nc.dram_tensor(name, shape, dt, kind="ExternalInput").ap()

    def dout(name, shape, dt=F32):
        return nc.dram_tensor(name, shape, dt, kind="ExternalOutput").ap()

    def dint(name, shape, dt=F32):
        return nc.dram_tensor(name, shape, dt, kind="Internal").ap()

    x_fm = din("x_fm", [BL, D, L])
    w_inT = din("w_inT", [D, D])
    cwT = din("cwT", [3, D, D])
    w_outT = din("w_outT", [D, D])
    w_BT_d = din("w_BT", [D, NST])
    w_CT_d = din("w_CT", [D, NST])
    w_d1T_d = din("w_d1T", [D, ND])
    w_d2T_d = din("w_d2T", [ND, D])
    rW = din("rW", [D, NE])
    w1d = din("w1", [NE, D, DFF])
    w3d = din("w3", [NE, D, DFF])
    w2d = din("w2", [NE, DFF, D])
    bin_c = din("bin_c", [128, NDT])
    cb_c = din("cb_c", [128, NDT])
    bd2_c = din("bd2_c", [128, NDT])
    bout_c = din("bout_c", [128, NDT])
    dpar_c = din("dpar_c", [128, NDT])
    alog_c = din("alog_c", [128, NDT * NST])
    bB_d = din("bB", [NST, 1])
    bC_d = din("bC", [NST, 1])
    bd1_d = din("bd1", [ND, 1])
    rb_d = din("rb", [1, NE])
    identc = din("identc", [128, 128])
    sel16c = din("sel16c", [NST, NST * 128])
    sel4c = din("sel4c", [NE, NE * 128])
    onesr = din("onesr", [1, 128])
    onesc = din("onesc", [128, 1])
    cbias = din("cbias", [128, 2])

    out_fm = dout("out_fm", [BL, D, L])
    stg1_d = dint("stg1_d", [BL, D, L])
    h0_d = dint("h0_d", [BL, 128, NDT, L], F16)
    w_inc_d = dint("w_inc_d", [128, NDT, D], F16)
    w_outc_d = dint("w_outc_d", [128, NDT, D], F16)
    cwc_d = dint("cwc_d", [128, NDT, 3, NDT, 128], F16)
    w13c_d = dint("w13c_d", [NE, NFB, 2, NDT, 128, 128], F16)
    w2c_d = dint("w2c_d", [NE, NFS, NDT, NKF, 128, 128], F16)

    es = ExitStack()
    E = Emit()
    with es:
        def sb(name, shape, dt):
            return es.enter_context(nc.sbuf_tensor(name, shape, dt))

        def psum(name, shape, dt=F32):
            return es.enter_context(nc.psum_tensor(name, shape, dt))

        wio = sb("wio", [128, NDT * NDT * 128], F16)
        xn16 = sb("xn16", [128, NDT * L], F16)
        hp16 = sb("hp16", [128, NDT * L], F16)
        hslot = sb("hslot", [128, NDT * (L + 2)], F16)
        w13b = [sb(f"w13b{i}", [128, 2 * NDT * 128], F16) for i in range(2)]
        w2b = [sb(f"w2b{i}", [128, 2 * NKF * 128], F16) for i in range(2)]
        xfa = sb("xfa", [128, L], F32)
        xfb = sb("xfb", [128, L], F32)
        ys16 = [sb(f"ys16_{i}", [128, L], F16) for i in range(2)]
        g16 = sb("g16", [128, L], F16)
        sq16 = sb("sq16", [128, L], F16)
        iob1 = sb("iob1", [128, L], F32)
        yb = sb("yb", [128, L], F32)
        h0c = [sb(f"h0c{i}", [128, L], F16) for i in range(2)]
        gatesFM = sb("gatesFM", [NE, L], F16)
        carry = sb("carry", [128, 128], F32)
        bmT = sb("bmT", [NST, CH], F16)
        cmT = sb("cmT", [NST, CH], F16)
        d1t = sb("d1t", [ND, CH], F16)
        c_sel16 = sb("c_sel16", [NST, NST * 128], F16)
        c_sel4 = sb("c_sel4", [NE, NE * 128], F16)
        c_rw16 = sb("c_rw16", [128, NDT * NE], F16)
        c_rb16 = sb("c_rb16", [1, NE], F16)
        c_ident = sb("c_ident", [128, 128], F32)
        c_onesr16 = sb("c_onesr16", [1, 128], F16)
        c_onesr32 = sb("c_onesr32", [1, 128], F32)
        c_onesc16 = sb("c_onesc16", [128, 1], F16)
        c_onesc32 = sb("c_onesc32", [128, 1], F32)
        c_bias = sb("c_bias", [128, 2], F32)
        p_negA = sb("p_negA", [128, NDT * NST], F32)
        p_bin = sb("p_bin", [128, NDT], F32)
        p_cb = sb("p_cb", [128, NDT], F32)
        p_bd2 = sb("p_bd2", [128, NDT], F32)
        p_bout = sb("p_bout", [128, NDT], F32)
        p_D = sb("p_D", [128, NDT], F32)
        p_bB = sb("p_bB", [NST, 1], F32)
        p_bC = sb("p_bC", [NST, 1], F32)
        p_bd1 = sb("p_bd1", [ND, 1], F32)
        p_sin = sb("p_sin", [128, NDT], F32)
        p_hbias = sb("p_hbias", [128, NDT], F32)
        p_rsb = sb("p_rsb", [128, 1], F32)
        p_mrs = sb("p_mrs", [128, 1], F32)
        wB_sb = sb("wB_sb", [128, NDT * NST], F16)
        wC_sb = sb("wC_sb", [128, NDT * NST], F16)
        wd1_sb = sb("wd1_sb", [128, NDT * ND], F16)
        wd2_sb = sb("wd2_sb", [ND, D], F16)
        st_x = sb("st_x", [128, 2 * NCH], F32)
        st_c1 = sb("st_c1", [128, NDT * NCH], F32)
        st_c2 = sb("st_c2", [128, NDT * NCH], F32)
        cn_rs2 = sb("cn_rs2", [128, NDT], F32)
        cn_bi2 = sb("cn_bi2", [128, NDT], F32)
        sc_t = sb("sc_t", [1, 16], F32)
        rows = sb("rows", [1, 4 * CH], F32)
        lgb = sb("lgb", [128, 8 * NE], F32)

        PU1 = [psum("PU1a", [128, TC]), psum("PU1b", [128, TC])]
        PU3 = [psum("PU3a", [128, TC]), psum("PU3b", [128, TC])]
        PW2 = [psum("PW2a", [128, TC]), psum("PW2b", [128, TC])]
        PS = [psum("PSa", [128, TC]), psum("PSb", [128, TC])]

        def v3(t, a, c):
            return t[:].rearrange("p (a c) -> p a c", a=a)

        xn3 = v3(xn16, NDT, L)
        hp3 = v3(hp16, NDT, L)
        h3 = hslot[:].rearrange("p (a c) -> p a c", a=NDT)
        wio4 = wio[:].rearrange("p (a b o) -> p a b o", a=NDT, b=NDT)
        he_t = [hslot[:, k * TC:(k + 1) * TC] for k in range(NKF)]
        ffnacc = [hslot[:, 4096 + i * 4096: 4096 + (i + 1) * 4096].rearrange(
            "p (a c) -> p a c", a=NDT) for i in range(2)]
        gb = hslot[:, 12288:12800]
        moeT = [hslot[:, 12800 + i * TC: 12800 + (i + 1) * TC] for i in range(2)]
        u3s = [hslot[:, 13824 + i * TC: 13824 + (i + 1) * TC] for i in range(2)]
        t16 = [hslot[:, 14848 + i * TC: 14848 + (i + 1) * TC] for i in range(2)]
        abar = [xfa[:].bitcast(F16)[:, 0:L].rearrange("p (a c) -> p a c", a=NDT),
                xfa[:].bitcast(F16)[:, L:2 * L].rearrange("p (a c) -> p a c", a=NDT)]
        bx3 = xfb[:].bitcast(F16)[:, 0:L].rearrange("p (a c) -> p a c", a=NDT)
        dh3 = xfb[:].bitcast(F16)[:, L:2 * L].rearrange("p (a c) -> p a c", a=NDT)
        zb3 = bx3
        dl3 = v3(iob1, NDT, CH)
        y3 = v3(yb, NDT, CH)
        g3 = v3(g16, NDT, CH)
        sq3 = v3(sq16, NDT, CH)
        ys3 = [v3(ys16[0], NDT, CH), v3(ys16[1], NDT, CH)]
        iob13 = v3(iob1, NDT, CH)
        w13b4 = [t[:].rearrange("p (w a o) -> p w a o", w=2, a=NDT) for t in w13b]
        w2b4 = [t[:].rearrange("p (w a o) -> p w a o", w=2, a=NKF) for t in w2b]

        cm = Ctr(es, nc, "cm")
        cv = Ctr(es, nc, "cv")
        cs = Ctr(es, nc, "cs")
        cp = Ctr(es, nc, "cp")
        cx = Ctr(es, nc, "cx", dma=True)
        cwm = Ctr(es, nc, "cwm", dma=True)
        csml = Ctr(es, nc, "csml", dma=True)
        cbigE = [Ctr(es, nc, f"cbigE{e}", dma=True) for e in range(NE)]
        cst = Ctr(es, nc, "cst", dma=True)
        s_x = [Ctr(es, nc, f"s_x{i}", dma=True) for i in range(2)]
        s_h0 = [Ctr(es, nc, f"s_h0{i}", dma=True) for i in range(2)]
        s_stg = Ctr(es, nc, "s_stg", dma=True)
        s_cw = [Ctr(es, nc, f"s_cw{i}", dma=True) for i in range(2)]
        s_w13 = [Ctr(es, nc, f"s_w13{i}", dma=True) for i in range(2)]
        s_w2 = [Ctr(es, nc, f"s_w2{i}", dma=True) for i in range(2)]
        s_wio = Ctr(es, nc, "s_wio", dma=True)
        st_out = Ctr(es, nc, "st_out", dma=True)

        block = es.enter_context(nc.Block())
        W = E.wait

        def sp_dma(out, in_, ctr, split=1):
            if split == 1:
                def f(e, out=out, in_=in_, c=ctr):
                    e.dma_start(out=out, in_=in_).then_inc(c.h, 16)
                E.do("sync", f)
                SCHED["sync"][-1] = ("i", ctr.name, 16)
                ctr.n += 16
                return (ctr, ctr.n)
            step = 128 // split
            for i in range(split):
                o = out[i * step:(i + 1) * step]
                ii = in_[i * step:(i + 1) * step]

                def f(e, o=o, ii=ii, c=ctr):
                    e.dma_start(out=o, in_=ii).then_inc(c.h, 16)
                E.do("sync", f)
                SCHED["sync"][-1] = ("i", ctr.name, 16)
                ctr.n += 16
            return (ctr, ctr.n)

        def gp_dma(out, in_, ctr=None, pre=True):
            c = ctr if ctr is not None else cwm
            if pre:
                E.wait("gpsimd", (c, c.n))

            def f(e, out=out, in_=in_, c=c):
                e.dma_start(out=out, in_=in_).then_inc(c.h, 16)
            E.do("gpsimd", f)
            SCHED["gpsimd"][-1] = ("i", c.name, 16)
            c.n += 16
            return (c, c.n)

        def sync_ser(out, in_):
            E.wait("sync", (cx, cx.n))

            def f(e, out=out, in_=in_):
                e.dma_start(out=out, in_=in_).then_inc(cx.h, 16)
            E.do("sync", f)
            SCHED["sync"][-1] = ("i", cx.name, 16)
            cx.n += 16
            return (cx, cx.n)

        gp_dma(w_inc_d, w_inT.rearrange("(a p) o -> p a o", p=128), ctr=csml)
        gp_dma(w_outc_d, w_outT.rearrange("(a p) o -> p a o", p=128), ctr=csml, pre=False)
        gp_dma(c_sel16[:], sel16c)
        gp_dma(c_sel4[:], sel4c, pre=False)
        gp_dma(c_onesc16[:], onesc, pre=False)
        gp_dma(c_onesr16[:], onesr, pre=False)
        gp_dma(c_rb16[:], rb_d, pre=False)
        gp_dma(c_rw16[:].rearrange("p (a e) -> p a e", a=NDT),
               rW.rearrange("(a p) e -> p a e", p=128), pre=False)
        gp_dma(wB_sb[:].rearrange("p (a n) -> p a n", a=NDT),
               w_BT_d.rearrange("(a p) n -> p a n", p=128), pre=False)
        gp_dma(wC_sb[:].rearrange("p (a n) -> p a n", a=NDT),
               w_CT_d.rearrange("(a p) n -> p a n", p=128), pre=False)
        gp_dma(wd1_sb[:].rearrange("p (a n) -> p a n", a=NDT),
               w_d1T_d.rearrange("(a p) n -> p a n", p=128), pre=False)
        ev_pre_g = gp_dma(wd2_sb[:], w_d2T_d, pre=False)
        E.do("gpsimd", lambda g: g.memset(h3[:, :, 0:1], 0.0))
        ev_gz0 = E.do("gpsimd", lambda g: g.memset(h3[:, :, L + 1:L + 2], 0.0), inc=cp, k=1)
        cw_src = cwT.rearrange("t (a p) (b o) -> p b t a o", p=128, o=128)
        for ot_ in range(NDT):
            for t_ in range(3):
                gp_dma(cwc_d[:, ot_, t_], cw_src[:, ot_, t_], ctr=csml, pre=False)
        ev_csml = (csml, csml.n)
        for e_ in range(NE):
            w1_src = w1d[e_].rearrange("(a p) (b o) -> b a p o", p=128, o=128)
            w3_src = w3d[e_].rearrange("(a p) (b o) -> b a p o", p=128, o=128)
            for q in range(NFB):
                gp_dma(w13c_d[e_, q, 0], w1_src[q], ctr=cbigE[e_], pre=False)
                gp_dma(w13c_d[e_, q, 1], w3_src[q], ctr=cbigE[e_], pre=False)
            w2_src = w2d[e_].rearrange("(f k p) (d o) -> d f k p o", k=NKF, p=128, o=128)
            for dt_ in range(NDT):
                for f_ in range(NFS):
                    gp_dma(w2c_d[e_, f_, dt_], w2_src[dt_, f_], ctr=cbigE[e_], pre=False)
        ev_cbigE = [(cbigE[e_], cbigE[e_].n) for e_ in range(NE)]
        sync_ser(c_ident[:], identc)
        sync_ser(c_onesr32[:], onesr)
        sync_ser(c_onesc32[:], onesc)
        sync_ser(p_negA[:], alog_c)
        sync_ser(p_bin[:], bin_c)
        sync_ser(p_cb[:], cb_c)
        sync_ser(p_bd2[:], bd2_c)
        sync_ser(p_bout[:], bout_c)
        sync_ser(p_D[:], dpar_c)
        sync_ser(p_bB[:], bB_d)
        sync_ser(p_bC[:], bC_d)
        sync_ser(p_bd1[:], bd1_d)
        ev_pre_x = sync_ser(c_bias[:], cbias)
        W("scalar", ev_pre_x)
        E.do("scalar", lambda s: s.activation(p_negA[:], p_negA[:], AF.Exp, scale=1.0))
        ev_negA = E.do("scalar", lambda s: s.mul(p_negA[:], p_negA[:], -1.0), inc=cs)

        W("sync", ev_csml)
        ev_wio = sp_dma(wio[:].rearrange("p (a c) -> p a c", a=NDT),
                        w_inc_d, s_wio, split=2)
        W("tensor", ev_wio)
        W("tensor", ev_pre_x)
        W("tensor", ev_pre_g)
        for ot in range(NDT):
            ev_mm = None
            for dt in range(NDT):
                ev_mm = E.do("tensor", lambda t, dt=dt, ot=ot: t.matmul(
                    PS[0][0:128, 0:1], wio4[:, dt, ot, :], c_onesc16[:],
                    start=(dt == 0), stop=(dt == NDT - 1)),
                    inc=(cm if dt == NDT - 1 else None))
            W("scalar", ev_mm)
            ev_sc = E.do("scalar", lambda s, ot=ot: s.copy(
                p_sin[:, ot:ot + 1], PS[0][0:128, 0:1]), inc=cs)
            W("tensor", ev_sc)

        moe_e_cast_ok = set()
        nbatch = 0 if stage == "pre" else (1 if stage in ("A", "A1", "A3", "B") else BL)
        for ib in range(nbatch):
            xb = x_fm[ib]
            if ib > 0:
                bars = [(cm, cm.n), (cv, cv.n), (cs, cs.n), (cp, cp.n),
                        (cst, cst.n), (st_out, st_out.n), (cwm, cwm.n), (cx, cx.n)]
                for sl in (s_x + s_h0 + [s_stg] + s_cw + s_w13 + s_w2 + [s_wio]):
                    bars.append((sl, sl.n))
                for eng in Emit.ENGS:
                    for b in bars:
                        W(eng, b)
                ev_wio = sp_dma(wio[:].rearrange("p (a c) -> p a c", a=NDT),
                                w_inc_d, s_wio, split=2)

            if ib == 0:
                ev_gz = ev_gz0
            else:
                ev_gz = E.do("gpsimd", lambda g: g.memset(h3[:, :, 0:1], 0.0))
                ev_gz = E.do("gpsimd", lambda g: g.memset(h3[:, :, L + 1:L + 2], 0.0), inc=cp, k=1)

            ev_stg_pipe = {}
            ev_router_last = None
            ev_bst = None
            ev_bc2 = None
            ev_mmst = None
            ev_stg = None
            ev_sq = None
            ev_var_prev = None
            for c in range(NCH):
                t0 = c * CH
                par = c % 2
                xf = (xfa if par == 0 else xfb)[:].rearrange("p (a c) -> p a c", a=NDT)
                xfr = (xfa if par == 0 else xfb)
                W("sync", ev_mmst)
                W("sync", ev_stg)
                ev_x32 = sp_dma(xf, xb[:, t0:t0 + CH].rearrange("(a p) c -> p a c", p=128),
                                s_x[par], split=4)
                W("scalar", ev_x32)
                W("scalar", ev_sq)
                W("scalar", ev_stg)
                ev_sq = E.do("scalar", lambda s, xfr=xfr: s.activation(
                    yb[:], xfr[:], AF.Square), inc=cs)
                W("tensor", ev_x32)
                W("tensor", ev_var_prev)
                for dt in range(NDT):
                    E.do("tensor", lambda t, dt=dt, xf=xf: t.matmul(
                        PU1[0][0:1, 0:CH], c_onesc32[:], xf[:, dt, :],
                        start=(dt == 0), stop=(dt == NDT - 1)))
                W("tensor", ev_sq)
                ev_mmst = None
                for dt in range(NDT):
                    ev_mmst = E.do("tensor", lambda t, dt=dt: t.matmul(
                        PU1[0][0:1, CH:2 * CH], c_onesc32[:], y3[:, dt, :],
                        start=(dt == 0), stop=(dt == NDT - 1)),
                        inc=(cm if dt == NDT - 1 else None))
                W("vector", ev_mmst)
                W("vector", ev_bc2)
                E.do("vector", lambda v: v.tensor_scalar(rows[0:1, 0:CH], PU1[0][0:1, 0:CH], 1.0 / D, None, ALU.mult))
                E.do("vector", lambda v: v.tensor_scalar(rows[0:1, CH:2 * CH], PU1[0][0:1, CH:2 * CH], 1.0 / D, None, ALU.mult))
                E.do("vector", lambda v: v.tensor_tensor(rows[0:1, 2 * CH:3 * CH], rows[0:1, 0:CH], rows[0:1, 0:CH], ALU.mult))
                ev_var = E.do("vector", lambda v: v.tensor_tensor(
                    rows[0:1, CH:2 * CH], rows[0:1, CH:2 * CH], rows[0:1, 2 * CH:3 * CH], ALU.subtract), inc=cv)
                ev_var_prev = ev_var
                W("scalar", ev_var)
                E.do("scalar", lambda s: s.activation(rows[0:1, CH:2 * CH], rows[0:1, CH:2 * CH], AF.Ln, bias=c_bias[0:1, 0:1]))
                ev_rs = E.do("scalar", lambda s: s.activation(
                    rows[0:1, CH:2 * CH], rows[0:1, CH:2 * CH], AF.Exp, scale=-0.5), inc=cs)
                W("vector", ev_rs)
                E.do("vector", lambda v: v.tensor_tensor(rows[0:1, 2 * CH:3 * CH], rows[0:1, 0:CH], rows[0:1, CH:2 * CH], ALU.mult))
                ev_m2 = E.do("vector", lambda v: v.tensor_scalar(
                    rows[0:1, 2 * CH:3 * CH], rows[0:1, 2 * CH:3 * CH], -1.0, None, ALU.mult), inc=cv)
                ev_rs1 = E.do("vector", lambda v: v.tensor_scalar(
                    rows[0:1, 3 * CH:4 * CH], rows[0:1, CH:2 * CH], 1.0, None, ALU.add), inc=cv)
                W("tensor", ev_m2)
                W("tensor", ev_rs1)
                E.do("tensor", lambda t: t.matmul(PU3[0][0:128, 0:CH], c_onesr32[:], rows[0:1, CH:2 * CH], start=True, stop=True))
                ev_bc1 = E.do("tensor", lambda t: t.matmul(PU3[0][0:128, CH:2 * CH], c_onesr32[:], rows[0:1, 2 * CH:3 * CH], start=True, stop=True), inc=cm)
                ev_bc2 = E.do("tensor", lambda t: t.matmul(PU3[1][0:128, 0:CH], c_onesr32[:], rows[0:1, 3 * CH:4 * CH], start=True, stop=True), inc=cm)
                W("vector", ev_bc1)
                bcast_rs = PU3[0][0:128, 0:CH].unsqueeze(1).to_broadcast([128, NDT, CH])
                bcast_m2 = PU3[0][0:128, CH:2 * CH].unsqueeze(1).to_broadcast([128, NDT, CH])
                E.do("vector", lambda v, xf=xf, b=bcast_rs: v.tensor_tensor(y3, xf, b, ALU.mult))
                ev_xn = E.do("vector", lambda v, t0=t0, b=bcast_m2: v.tensor_tensor(
                    xn3[:, :, t0:t0 + CH], y3, b, ALU.add), inc=cv)
                W("vector", ev_bc2)
                bcast_rs1 = PU3[1][0:128, 0:CH].unsqueeze(1).to_broadcast([128, NDT, CH])
                W("vector", ev_stg_pipe.get(c - 1))
                E.do("vector", lambda v, xf=xf, b=bcast_rs1: v.tensor_tensor(y3, xf, b, ALU.mult))
                ev_stg = E.do("vector", lambda v, b=bcast_m2: v.tensor_tensor(iob13, y3, b, ALU.add), inc=cv)
                W("sync", ev_stg)
                ev_stg_pipe[c] = sp_dma(stg1_d[ib][:, t0:t0 + CH].rearrange("(a p) c -> p a c", p=128),
                                        iob13, cst, split=2)
                W("scalar", ev_xn)
                E.do("scalar", lambda s, c=c, t0=t0: s.activation(
                    g16[:], xn3[:, :, t0:t0 + CH], AF.Identity, accum_out=st_x[:, c:c + 1]))
                ev_bst = E.do("scalar", lambda s, c=c, t0=t0: s.activation(
                    g16[:], xn3[:, :, t0:t0 + CH], AF.Square, accum_out=st_x[:, NCH + c:NCH + c + 1]), inc=cs)
                W("tensor", ev_xn)
                W("tensor", ev_stg)
                W("vector", ev_bst)

                ev_gfm = None
                for tt in range(2):
                    base = tt * NE
                    W("tensor", ev_xn)
                    for dt in range(NDT):
                        E.do("tensor", lambda t, dt=dt, tt=tt, base=base, t0=t0: t.matmul(
                            PS[0][0:128, base:base + NE],
                            xn3[:, dt, t0 + tt * 128: t0 + (tt + 1) * 128],
                            c_rw16[:, dt * NE:(dt + 1) * NE],
                            start=(dt == 0), stop=False))
                    ev_lg = E.do("tensor", lambda t, base=base: t.matmul(
                        PS[0][0:128, base:base + NE], c_onesr16[:], c_rb16[:],
                        start=False, stop=True), inc=cm)
                    W("vector", ev_lg)
                    s0_ = lgb[:, tt * 16 + 0: tt * 16 + 4]
                    s1_ = lgb[:, tt * 16 + 4: tt * 16 + 8]
                    s2_ = lgb[:, tt * 16 + 8: tt * 16 + 9]
                    s3_ = lgb[:, tt * 16 + 9: tt * 16 + 10]
                    s4_ = lgb[:, tt * 16 + 10: tt * 16 + 11]
                    ev_cp0 = E.do("vector", lambda v, s0_=s0_, base=base: v.tensor_copy(
                        s0_, PS[0][0:128, base:base + NE]), inc=cv)
                    W("tensor", ev_cp0)
                    ev_neg = E.do("vector", lambda v, s0_=s0_, s2_=s2_: v.tensor_reduce(
                        s2_, s0_, AX.X, ALU.max, negate=True), inc=cv)
                    W("scalar", ev_neg)
                    ev_exp = E.do("scalar", lambda s, s0_=s0_, s2_=s2_: s.activation(
                        s0_, s0_, AF.Exp, bias=s2_), inc=cs)
                    W("vector", ev_exp)
                    E.do("vector", lambda v, s0_=s0_, s2_=s2_: v.tensor_reduce(s2_, s0_, AX.X, ALU.add))
                    E.do("vector", lambda v, s2_=s2_, s3_=s3_: v.reciprocal(s3_, s2_))
                    E.do("vector", lambda v, s0_=s0_, s3_=s3_: v.tensor_scalar(s0_, s0_, s3_, None, ALU.mult))
                    E.do("vector", lambda v, s0_=s0_, s2_=s2_: v.tensor_reduce(s2_, s0_, AX.X, ALU.max))
                    E.do("vector", lambda v, s0_=s0_, s1_=s1_, s2_=s2_: v.tensor_scalar(s1_, s0_, s2_, None, ALU.is_equal))
                    E.do("vector", lambda v, s0_=s0_, s1_=s1_: v.scalar_tensor_tensor(s1_, s1_, -10.0, s0_, ALU.mult, ALU.add))
                    E.do("vector", lambda v, s1_=s1_, s3_=s3_: v.tensor_reduce(s3_, s1_, AX.X, ALU.max))
                    E.do("vector", lambda v, s2_=s2_, s3_=s3_, s4_=s4_: v.tensor_tensor(s4_, s2_, s3_, ALU.add))
                    E.do("vector", lambda v, s4_=s4_: v.reciprocal(s4_, s4_))
                    E.do("vector", lambda v, s0_=s0_, s1_=s1_, s3_=s3_: v.tensor_scalar(s1_, s0_, s3_, None, ALU.is_ge))
                    E.do("vector", lambda v, s0_=s0_, s4_=s4_: v.tensor_scalar(s0_, s0_, s4_, None, ALU.mult))
                    ev_gt = E.do("vector", lambda v, s0_=s0_, s1_=s1_: v.tensor_tensor(s0_, s0_, s1_, ALU.mult), inc=cv)
                    W("tensor", ev_gt)
                    ev_tr = E.do("tensor", lambda t, s0_=s0_: t.transpose(
                        PS[1][0:NE, 128:256], s0_, c_ident[:]), inc=cm)
                    W("scalar", ev_tr)
                    ev_gfm = E.do("scalar", lambda s, tt=tt, t0=t0: s.copy(
                        gatesFM[0:NE, t0 + tt * 128: t0 + (tt + 1) * 128],
                        PS[1][0:NE, 128:256]), inc=cs)
                    W("tensor", ev_gfm)
                    W("vector", ev_gfm)
                ev_router_last = ev_gfm

            if stage == "A1":
                continue
            ev_r1 = E.do("vector", lambda v: v.tensor_reduce(st_x[:, 0:1], st_x[:, 0:NCH], AX.X, ALU.add), inc=cv)
            ev_r2 = E.do("vector", lambda v: v.tensor_reduce(st_x[:, NCH:NCH + 1], st_x[:, NCH:2 * NCH], AX.X, ALU.add), inc=cv)
            W("tensor", ev_r1)
            W("tensor", ev_r2)
            ev_s1 = E.do("tensor", lambda t: t.matmul(PU1[0][0:1, 0:1], st_x[:, 0:1], c_onesc32[:], start=True, stop=True), inc=cm)
            ev_s2 = E.do("tensor", lambda t: t.matmul(PU1[0][0:1, 1:2], st_x[:, NCH:NCH + 1], c_onesc32[:], start=True, stop=True), inc=cm)
            W("vector", ev_s2)
            scv = 1.0 / (L * D)
            E.do("vector", lambda v: v.tensor_scalar(sc_t[0:1, 0:1], PU1[0][0:1, 0:1], scv, None, ALU.mult))
            E.do("vector", lambda v: v.tensor_scalar(sc_t[0:1, 1:2], PU1[0][0:1, 1:2], scv, None, ALU.mult))
            E.do("vector", lambda v: v.tensor_tensor(sc_t[0:1, 2:3], sc_t[0:1, 0:1], sc_t[0:1, 0:1], ALU.mult))
            ev_v = E.do("vector", lambda v: v.tensor_tensor(sc_t[0:1, 1:2], sc_t[0:1, 1:2], sc_t[0:1, 2:3], ALU.subtract), inc=cv)
            W("scalar", ev_v)
            E.do("scalar", lambda s: s.activation(sc_t[0:1, 1:2], sc_t[0:1, 1:2], AF.Ln, bias=c_bias[0:1, 0:1]))
            ev_rsb = E.do("scalar", lambda s: s.activation(sc_t[0:1, 1:2], sc_t[0:1, 1:2], AF.Exp, scale=-0.5), inc=cs)
            W("vector", ev_rsb)
            E.do("vector", lambda v: v.tensor_tensor(sc_t[0:1, 3:4], sc_t[0:1, 0:1], sc_t[0:1, 1:2], ALU.mult))
            ev_mrs = E.do("vector", lambda v: v.tensor_scalar(sc_t[0:1, 3:4], sc_t[0:1, 3:4], -1.0, None, ALU.mult), inc=cv)
            W("tensor", ev_mrs)
            ev_b1 = E.do("tensor", lambda t: t.matmul(PU1[0][0:128, 2:3], c_onesr32[:], sc_t[0:1, 1:2], start=True, stop=True), inc=cm)
            ev_b2 = E.do("tensor", lambda t: t.matmul(PU1[0][0:128, 3:4], c_onesr32[:], sc_t[0:1, 3:4], start=True, stop=True), inc=cm)
            W("scalar", ev_b1)
            W("scalar", ev_b2)
            E.do("scalar", lambda s: s.copy(p_rsb[:], PU1[0][0:128, 2:3]))
            ev_mrsc = E.do("scalar", lambda s: s.copy(p_mrs[:], PU1[0][0:128, 3:4]), inc=cs)
            W("vector", ev_mrsc)
            ev_hb = E.do("vector", lambda v: v.scalar_tensor_tensor(
                p_hbias[:], p_sin[:], p_mrs[:, 0:1], p_bin[:], ALU.mult, ALU.add), inc=cv)
            W("tensor", ev_hb)
            W("scalar", ev_hb)

            W("tensor", ev_wio)
            W("tensor", ev_gz)
            ev_hev = None
            for c in range(NCH):
                t0 = c * CH
                for ot in range(NDT):
                    pben = PW2[(c * NDT + ot) % 2]
                    ev_mm = None
                    for dt in range(NDT):
                        ev_mm = E.do("tensor", lambda t, dt=dt, ot=ot, pben=pben, t0=t0: t.matmul(
                            pben[0:128, 0:CH], wio4[:, dt, ot, :],
                            xn3[:, dt, t0:t0 + CH], start=(dt == 0), stop=(dt == NDT - 1)),
                            inc=(cm if dt == NDT - 1 else None))
                    W("scalar", ev_mm)
                    ev_hev = E.do("scalar", lambda s, ot=ot, pben=pben, t0=t0: s.activation(
                        h3[:, ot, t0 + 1: t0 + 1 + CH], pben[0:128, 0:CH], AF.Identity,
                        scale=p_rsb[:, 0:1], bias=p_hbias[:, ot:ot + 1]), inc=cs)
                    W("tensor", ev_hev)
                W("sync", ev_hev)
                sp_dma(h0_d[ib][:, :, t0:t0 + CH], h3[:, :, t0 + 1:t0 + 1 + CH], cst, split=2)

            if stage == "A3":
                continue
            W("sync", ev_hev)
            ev_wout = sp_dma(wio[:].rearrange("p (a c) -> p a c", a=NDT),
                             w_outc_d, s_wio, split=2)

            ev_st2 = None
            conv_mm_ev = {}
            for ot in range(NDT):
                slot = ot % 2
                W("sync", conv_mm_ev.get(slot))
                ev_cwA = sp_dma(w13b[slot][:].rearrange("p (w a o) -> p w a o", w=2, a=NDT)[:, 0],
                                cwc_d[:, ot, 0], s_cw[slot])
                ev_cwA = sp_dma(w13b[slot][:].rearrange("p (w a o) -> p w a o", w=2, a=NDT)[:, 1],
                                cwc_d[:, ot, 1], s_cw[slot])
                ev_cwB = sp_dma(w2b[slot][:, 0:NDT * 128].rearrange("p (a o) -> p a o", a=NDT),
                                cwc_d[:, ot, 2], s_cw[slot])
                W("tensor", ev_cwA)
                W("tensor", ev_cwB)
                for c in range(NCH):
                    t0 = c * CH
                    pben = PW2[c % 2]
                    ev_mm = None
                    k = 0
                    for tau in range(3):
                        for it in range(NDT):
                            if tau < 2:
                                lhs = w13b4[slot][:, tau, it, :]
                            else:
                                lhs = w2b4[slot][:, 0, it, :]
                            ev_mm = E.do("tensor", lambda t, lhs=lhs, tau=tau, it=it, k=k, pben=pben, t0=t0: t.matmul(
                                pben[0:128, 0:CH], lhs,
                                h3[:, it, t0 + tau: t0 + tau + CH],
                                start=(k == 0), stop=(k == 23)),
                                inc=(cm if k == 23 else None))
                            k += 1
                    conv_mm_ev[slot] = ev_mm
                    W("scalar", ev_mm)
                    col = ot * NCH + c
                    E.do("scalar", lambda s, pben=pben, ot=ot, t0=t0, col=col: s.activation(
                        hp3[:, ot, t0:t0 + CH], pben[0:128, 0:CH], AF.Identity,
                        bias=p_cb[:, ot:ot + 1], accum_out=st_c1[:, col:col + 1]))
                    ev_st2 = E.do("scalar", lambda s, pben=pben, ot=ot, col=col: s.activation(
                        g16[:, 0:CH], pben[0:128, 0:CH], AF.Square,
                        bias=p_cb[:, ot:ot + 1], accum_out=st_c2[:, col:col + 1]), inc=cs)
                    W("tensor", ev_st2)
            W("vector", ev_st2)
            E.do("vector", lambda v: v.tensor_reduce(
                cn_rs2[:], st_c1[:].rearrange("p (a c) -> p a c", a=NDT), AX.X, ALU.add))
            ev_c2 = E.do("vector", lambda v: v.tensor_reduce(
                cn_bi2[:], st_c2[:].rearrange("p (a c) -> p a c", a=NDT), AX.X, ALU.add), inc=cv)
            E.do("vector", lambda v: v.tensor_scalar(cn_rs2[:], cn_rs2[:], 1.0 / L, None, ALU.mult))
            E.do("vector", lambda v: v.tensor_scalar(cn_bi2[:], cn_bi2[:], 1.0 / L, None, ALU.mult))
            E.do("vector", lambda v: v.tensor_tensor(st_c1[:, 0:NDT], cn_rs2[:], cn_rs2[:], ALU.mult))
            ev_cv_ = E.do("vector", lambda v: v.tensor_tensor(cn_bi2[:], cn_bi2[:], st_c1[:, 0:NDT], ALU.subtract), inc=cv)
            W("scalar", ev_cv_)
            E.do("scalar", lambda s: s.activation(cn_bi2[:], cn_bi2[:], AF.Ln, bias=c_bias[:, 0:1]))
            ev_crs = E.do("scalar", lambda s: s.activation(cn_bi2[:], cn_bi2[:], AF.Exp, scale=-0.5), inc=cs)
            W("vector", ev_crs)
            E.do("vector", lambda v: v.tensor_tensor(st_c1[:, 0:NDT], cn_rs2[:], cn_bi2[:], ALU.mult))
            E.do("vector", lambda v: v.tensor_scalar(cn_rs2[:], cn_bi2[:], 0.5, None, ALU.mult))
            ev_convfin = E.do("vector", lambda v: v.tensor_scalar(cn_bi2[:], st_c1[:, 0:NDT], -0.5, None, ALU.mult), inc=cv)

            if stage in ("A",):
                continue
            W("sync", (cst, cst.n))
            W("gpsimd", (cst, cst.n))
            W("scalar", (cst, cst.n))
            W("vector", (cst, cst.n))
            ev_h0l = {}
            ev_h0l[0] = sp_dma(h0c[0][:].rearrange("p (a c) -> p a c", a=NDT),
                               h0_d[ib][:, :, 0:CH], s_h0[0], split=2)

            ev_conv_last_mm = (cm, cm.n)
            evB_cs = (cs, cs.n)
            evB_cv = (cv, cv.n)
            moe_started = [False]
            ev_he = {}
            ev_tm = {}
            ev_ffn_t = {}
            ev_ffn_w = {0: None, 1: None}
            ev_s5_done = {0: None, 1: None}
            ev_ff_memset = {}
            ev_u_sl = {0: None, 1: None}
            ev_he_sl = {0: None, 1: None}
            ev_w13_mm = {0: conv_mm_ev.get(0), 1: conv_mm_ev.get(1)}
            ev_w2_mm = {0: None, 1: None,
                        ("sl", 0): conv_mm_ev.get(0), ("sl", 1): conv_mm_ev.get(1)}
            ev_gbev = [None]

            def moe_gb_seg(tc, e):
                W("tensor", ev_router_last)
                W("tensor", ev_conv_last_mm)
                W("tensor", evB_cs)
                W("tensor", evB_cv)
                W("tensor", ev_tm.get(0))
                ev_g1 = E.do("tensor", lambda t, tc=tc, e=e: t.matmul(
                    PW2[0][0:128, 0:TC], c_sel4[0:NE, e * 128:(e + 1) * 128],
                    gatesFM[0:NE, tc * TC:(tc + 1) * TC], start=True, stop=True), inc=cm)
                W("scalar", ev_g1)
                W("scalar", ev_tm.get(0))
                W("scalar", ev_tm.get(1))
                ev_gbev[0] = E.do("scalar", lambda s: s.copy(gb, PW2[0][0:128, 0:TC]), inc=cs)
                W("tensor", ev_gbev[0])

            def moe_kf_seg(tc, e, fs, kf):
                slot = kf % 2
                par = kf % 2
                fb = fs * NKF + kf
                W("sync", ev_w13_mm[slot])
                if e not in moe_e_cast_ok:
                    W("sync", ev_cbigE[e])
                    moe_e_cast_ok.add(e)
                ev_d = sp_dma(w13b[slot][:].rearrange("p (w a o) -> p w a o", w=2, a=NDT),
                              w13c_d[e, fb].rearrange("w a p o -> p w a o"), s_w13[slot], split=4)
                W("tensor", ev_d)
                if not moe_started[0]:
                    W("tensor", ev_conv_last_mm)
                    W("tensor", evB_cs)
                    W("tensor", evB_cv)
                    moe_started[0] = True
                W("tensor", ev_u_sl[par])
                ev_mm1 = None
                for dt in range(NDT):
                    ev_mm1 = E.do("tensor", lambda t, dt=dt, slot=slot, par=par, tc=tc: t.matmul(
                        PU1[par][0:128, 0:TC], w13b4[slot][:, 0, dt, :],
                        xn3[:, dt, tc * TC:(tc + 1) * TC],
                        start=(dt == 0), stop=(dt == NDT - 1)),
                        inc=(cm if dt == NDT - 1 else None))
                ev_mm3 = None
                for dt in range(NDT):
                    ev_mm3 = E.do("tensor", lambda t, dt=dt, slot=slot, par=par, tc=tc: t.matmul(
                        PU3[par][0:128, 0:TC], w13b4[slot][:, 1, dt, :],
                        xn3[:, dt, tc * TC:(tc + 1) * TC],
                        start=(dt == 0), stop=(dt == NDT - 1)),
                        inc=(cm if dt == NDT - 1 else None))
                ev_w13_mm[slot] = ev_mm3
                W("scalar", ev_mm1)
                W("scalar", ev_he_sl[par])
                E.do("scalar", lambda s, par=par: s.activation(
                    moeT[par], PU1[par][0:128, 0:TC], AF.Silu))
                W("scalar", ev_mm3)
                ev_u = E.do("scalar", lambda s, par=par: s.copy(
                    u3s[par], PU3[par][0:128, 0:TC]), inc=cs)
                ev_u_sl[par] = ev_u
                W("gpsimd", ev_u)
                W("gpsimd", ev_w2_mm.get("last"))
                W("gpsimd", ev_conv_last_mm)
                ev_h = E.do("gpsimd", lambda g, par=par, kf=kf: g.tensor_tensor(
                    he_t[kf], moeT[par], u3s[par], ALU.mult), inc=cp)
                ev_he[kf] = ev_h
                ev_he_sl[par] = ev_h

            def moe_dt_seg(tc, e, fs, dt):
                slot = (dt // 2) % 2
                bank = dt % 2
                tcpar = tc % 2
                if dt % 2 == 0:
                    W("sync", ev_w2_mm.get(("sl", slot)))
                    ev_d = sp_dma(w2b[slot][:].rearrange("p (w a o) -> p w a o", w=2, a=NKF),
                                  w2c_d[e, fs, dt:dt + 2].rearrange("d k p o -> p d k o"),
                                  s_w2[slot], split=2)
                    ev_w2_mm[("dma", slot)] = ev_d
                if dt == 0:
                    for kf in range(NKF):
                        W("tensor", ev_he[kf])
                W("tensor", ev_w2_mm[("dma", slot)])
                W("tensor", ev_tm.get(bank))
                W("tensor", ev_gbev[0])
                ev_mm = None
                for kf in range(NKF):
                    ev_mm = E.do("tensor", lambda t, kf=kf, slot=slot, bank=bank, dt=dt: t.matmul(
                        PW2[bank][0:128, 0:TC], w2b4[slot][:, dt % 2, kf, :], he_t[kf],
                        start=(kf == 0), stop=(kf == NKF - 1)),
                        inc=(cm if kf == NKF - 1 else None))
                ev_w2_mm[("sl", slot)] = ev_mm
                ev_w2_mm["last"] = ev_mm
                W("vector", ev_mm)
                W("vector", ev_gbev[0])
                W("vector", ev_ffn_t.get(bank))
                ev_t = E.do("vector", lambda v, bank=bank: v.tensor_tensor(
                    t16[bank], PW2[bank][0:128, 0:TC], gb, ALU.mult), inc=cv)
                ev_tm[bank] = ev_t
                W("gpsimd", ev_t)
                W("gpsimd", ev_ff_memset.get(tc))
                ev_f = E.do("gpsimd", lambda g, bank=bank, dt=dt, tcpar=tcpar: g.tensor_tensor(
                    ffnacc[tcpar][:, dt, :], ffnacc[tcpar][:, dt, :], t16[bank], ALU.add), inc=cp)
                ev_ffn_t[bank] = ev_f
                ev_ffn_w[tcpar] = ev_f

            def moe_unit_segs(tc, e, fs):
                segs = []
                if fs == 0:
                    segs.append(lambda tc=tc, e=e: moe_gb_seg(tc, e))
                for kf in range(NKF):
                    segs.append(lambda tc=tc, e=e, fs=fs, kf=kf: moe_kf_seg(tc, e, fs, kf))
                for dt in range(NDT):
                    segs.append(lambda tc=tc, e=e, fs=fs, dt=dt: moe_dt_seg(tc, e, fs, dt))
                return segs

            def ffn_memset_seg(tc):
                tcpar = tc % 2
                W("gpsimd", ev_s5_done[tcpar])
                W("gpsimd", ev_conv_last_mm)
                ev = E.do("gpsimd", lambda g, tcpar=tcpar: g.memset(
                    ffnacc[tcpar][:, :, :], 0.0), inc=cp)
                ev_ff_memset[tc] = ev

            st = {}

            def scan_head_seg(c):
                t0 = c * CH
                if c + 1 < NCH:
                    W("sync", st.get(("h0_free", (c + 1) % 2)))
                    ev_h0l[c + 1] = sp_dma(
                        h0c[(c + 1) % 2][:].rearrange("p (a c) -> p a c", a=NDT),
                        h0_d[ib][:, :, (c + 1) * CH:(c + 2) * CH], s_h0[(c + 1) % 2], split=2)
                W("vector", ev_convfin)
                ev_z = None
                for ft in range(NDT):
                    ev_z = E.do("vector", lambda v, ft=ft, t0=t0: v.tensor_scalar(
                        zb3[:, ft, :], hp3[:, ft, t0:t0 + CH],
                        cn_rs2[:, ft:ft + 1], cn_bi2[:, ft:ft + 1], ALU.mult, ALU.add),
                        inc=(cv if ft == NDT - 1 else None))
                W("scalar", ev_z)
                W("scalar", st.get("ev_abar_free"))
                ev_t = E.do("scalar", lambda s: s.activation(
                    abar[0][:, :, :], zb3, AF.Tanh), inc=cs)
                W("vector", ev_t)
                ev_hp = E.do("vector", lambda v, t0=t0: v.scalar_tensor_tensor(
                    hp3[:, :, t0:t0 + CH], abar[0][:, :, :], 1.0, zb3, ALU.add, ALU.mult), inc=cv)
                st["ev_hp"] = ev_hp
                st["yadd_prev"] = st.get("yadd_ev", {})
                W("tensor", ev_hp)
                ev_mmB = None
                for dt in range(NDT):
                    ev_mmB = E.do("tensor", lambda t, dt=dt, t0=t0: t.matmul(
                        PS[0][0:NST, 0:CH], wB_sb[:, dt * NST:(dt + 1) * NST],
                        hp3[:, dt, t0:t0 + CH], start=(dt == 0), stop=(dt == NDT - 1)),
                        inc=(cm if dt == NDT - 1 else None))
                ev_mmC = None
                for dt in range(NDT):
                    ev_mmC = E.do("tensor", lambda t, dt=dt, t0=t0: t.matmul(
                        PS[1][0:NST, 0:CH], wC_sb[:, dt * NST:(dt + 1) * NST],
                        hp3[:, dt, t0:t0 + CH], start=(dt == 0), stop=(dt == NDT - 1)),
                        inc=(cm if dt == NDT - 1 else None))
                W("scalar", ev_mmB)
                ev_bm = E.do("scalar", lambda s: s.activation(bmT[:], PS[0][0:NST, 0:CH], AF.Identity, bias=p_bB[:, 0:1]), inc=cs)
                W("scalar", ev_mmC)
                ev_cm2 = E.do("scalar", lambda s: s.activation(cmT[:], PS[1][0:NST, 0:CH], AF.Identity, bias=p_bC[:, 0:1]), inc=cs)
                W("tensor", ev_bm)
                ev_mmd1 = None
                for dt in range(NDT):
                    ev_mmd1 = E.do("tensor", lambda t, dt=dt, t0=t0: t.matmul(
                        PS[0][0:ND, 0:CH], wd1_sb[:, dt * ND:(dt + 1) * ND],
                        hp3[:, dt, t0:t0 + CH], start=(dt == 0), stop=(dt == NDT - 1)),
                        inc=(cm if dt == NDT - 1 else None))
                W("scalar", ev_mmd1)
                ev_d1 = E.do("scalar", lambda s: s.activation(d1t[:], PS[0][0:ND, 0:CH], AF.Identity, bias=p_bd1[:, 0:1]), inc=cs)
                W("tensor", ev_d1)
                W("tensor", ev_cm2)
                W("scalar", st.get("ev_g_prev"))
                W("scalar", st.get("ev_s5_comb"))
                W("scalar", st.get("ev_s5_store"))
                ev_dl = None
                for ot in range(NDT):
                    pben = PS[ot % 2]
                    ev_d2 = E.do("tensor", lambda t, ot=ot, pben=pben: t.matmul(
                        pben[0:128, 256:512], wd2_sb[:, ot * 128:(ot + 1) * 128], d1t[:],
                        start=True, stop=True), inc=cm)
                    W("scalar", ev_d2)
                    E.do("scalar", lambda s, pben=pben, ot=ot: s.activation(
                        yb[:, 0:CH], pben[0:128, 256:512], AF.Exp, bias=p_bd2[:, ot:ot + 1]))
                    ev_dl = E.do("scalar", lambda s, ot=ot: s.activation(
                        dl3[:, ot, :], yb[:, 0:CH], AF.Ln, bias=c_bias[:, 1:2]), inc=cs)
                    W("tensor", ev_dl)
                st["ev_dl"] = ev_dl
                W("vector", ev_dl)
                ev_dh = E.do("vector", lambda v, t0=t0: v.tensor_tensor(
                    dh3, dl3, hp3[:, :, t0:t0 + CH], ALU.mult), inc=cv)
                st["ev_dh"] = ev_dh
                st["rep_ev"] = {}
                st["yprod_ev"] = {}
                st["yadd_ev"] = {}

            def scan_n_seg(c, n):
                t0 = c * CH
                par = n % 2
                pb = PS[par]
                W("tensor", st["yprod_ev"].get(n - 2))
                if n <= 1:
                    W("tensor", st["ev_dl"])
                ev_rB = E.do("tensor", lambda t, n=n, pb=pb: t.matmul(
                    pb[0:128, 0:CH], c_sel16[0:NST, n * 128:(n + 1) * 128], bmT[:],
                    start=True, stop=True), inc=cm)
                ev_rC = E.do("tensor", lambda t, n=n, pb=pb: t.matmul(
                    pb[0:128, CH:2 * CH], c_sel16[0:NST, n * 128:(n + 1) * 128], cmT[:],
                    start=True, stop=True), inc=cm)
                W("scalar", st["yprod_ev"].get(n - 2))
                if n <= 1:
                    W("scalar", st["ev_hp"])
                    W("scalar", st.get("ev_abar_free"))
                ev_ab = None
                for ft in range(NDT):
                    ev_ab = E.do("scalar", lambda s, ft=ft, n=n, par=par: s.activation(
                        abar[par][:, ft, :], dl3[:, ft, :], AF.Exp,
                        scale=p_negA[:, ft * NST + n: ft * NST + n + 1]),
                        inc=(cs if ft == NDT - 1 else None))
                W("vector", ev_rB)
                W("vector", st["ev_dh"])
                ev_bx = E.do("vector", lambda v, pb=pb: v.tensor_tensor(
                    bx3, dh3, pb[0:128, 0:CH].unsqueeze(1).to_broadcast([128, NDT, CH]),
                    ALU.mult), inc=cv)
                W("vector", ev_ab)
                ev_sc2 = None
                for ft in range(NDT):
                    col = n * NDT + ft
                    if c == 0:
                        ev_sc2 = E.do("vector", lambda v, ft=ft, par=par: v.tensor_tensor_scan(
                            abar[par][:, ft, :], abar[par][:, ft, :], bx3[:, ft, :],
                            0.0, ALU.mult, ALU.add), inc=(cv if ft == NDT - 1 else None))
                    else:
                        ev_sc2 = E.do("vector", lambda v, ft=ft, par=par, col=col: v.tensor_tensor_scan(
                            abar[par][:, ft, :], abar[par][:, ft, :], bx3[:, ft, :],
                            carry[:, col:col + 1], ALU.mult, ALU.add),
                            inc=(cv if ft == NDT - 1 else None))
                if c < NCH - 1:
                    E.do("vector", lambda v, n=n, par=par: v.tensor_copy(
                        carry[:, n * NDT:(n + 1) * NDT],
                        abar[par][:, :, CH - 1:CH].rearrange("p a c -> p (a c)")))
                prev_ya = st["yadd_ev"].get(n - 2)
                if prev_ya is None and n < 2:
                    prev_ya = st.get("yadd_prev", {}).get(NST - 2 + n)
                W("vector", prev_ya)
                W("vector", ev_rC)
                ev_yp = E.do("vector", lambda v, pb=pb, par=par: v.tensor_tensor(
                    ys3[par], abar[par][:, :, :],
                    pb[0:128, CH:2 * CH].unsqueeze(1).to_broadcast([128, NDT, CH]),
                    ALU.mult), inc=cv)
                st["yprod_ev"][n] = ev_yp
                W("gpsimd", ev_yp)
                if n == 0:
                    ev_ya = E.do("gpsimd", lambda g, par=par: g.tensor_copy(
                        y3[:, :, :], ys3[par]), inc=cp)
                else:
                    ev_ya = E.do("gpsimd", lambda g, par=par: g.tensor_tensor(
                        y3[:, :, :], y3[:, :, :], ys3[par], ALU.add), inc=cp)
                st["yadd_ev"][n] = ev_ya

            def scan_tail_seg(c):
                t0 = c * CH
                cpar = c % 2
                W("vector", st["yadd_ev"][NST - 1])
                ev_yd = None
                for ft in range(NDT):
                    ev_yd = E.do("vector", lambda v, ft=ft, t0=t0: v.scalar_tensor_tensor(
                        y3[:, ft, :], hp3[:, ft, t0:t0 + CH], p_D[:, ft:ft + 1],
                        y3[:, ft, :], ALU.mult, ALU.add),
                        inc=(cv if ft == NDT - 1 else None))
                W("scalar", ev_h0l[c])
                W("scalar", st["yprod_ev"].get(NST - 2))
                ev_t2 = E.do("scalar", lambda s, cpar=cpar: s.activation(
                    abar[0][:, :, :], v3(h0c[cpar], NDT, CH), AF.Tanh, scale=0.5), inc=cs)
                W("vector", ev_t2)
                ev_s0 = E.do("vector", lambda v, cpar=cpar: v.scalar_tensor_tensor(
                    abar[1][:, :, :], abar[0][:, :, :], 1.0, v3(h0c[cpar], NDT, CH),
                    ALU.add, ALU.mult), inc=cv)
                st[("h0_free", cpar)] = ev_s0
                ev_g = E.do("vector", lambda v: v.scalar_tensor_tensor(
                    g3, abar[1][:, :, :], 0.5, y3[:, :, :], ALU.mult, ALU.mult), inc=cv)
                st["ev_g_prev"] = ev_g
                st["ev_abar_free"] = ev_g
                W("scalar", ev_g)
                W("scalar", st.get("ev_fa_prev"))
                ev_sq2 = E.do("scalar", lambda s: s.activation(sq3, g3, AF.Square), inc=cs)
                W("tensor", st["yprod_ev"][NST - 1])
                W("tensor", ev_g)
                for dt in range(NDT):
                    E.do("tensor", lambda t, dt=dt: t.matmul(
                        PS[0][0:1, 0:CH], c_onesc16[:], g3[:, dt, :],
                        start=(dt == 0), stop=(dt == NDT - 1)))
                W("tensor", ev_sq2)
                ev_mg = None
                for dt in range(NDT):
                    ev_mg = E.do("tensor", lambda t, dt=dt: t.matmul(
                        PS[0][0:1, CH:2 * CH], c_onesc16[:], sq3[:, dt, :],
                        start=(dt == 0), stop=(dt == NDT - 1)),
                        inc=(cm if dt == NDT - 1 else None))
                W("vector", ev_mg)
                E.do("vector", lambda v: v.tensor_scalar(rows[0:1, 0:CH], PS[0][0:1, 0:CH], 1.0 / D, None, ALU.mult))
                E.do("vector", lambda v: v.tensor_scalar(rows[0:1, CH:2 * CH], PS[0][0:1, CH:2 * CH], 1.0 / D, None, ALU.mult))
                E.do("vector", lambda v: v.tensor_tensor(rows[0:1, 2 * CH:3 * CH], rows[0:1, 0:CH], rows[0:1, 0:CH], ALU.mult))
                ev_gv = E.do("vector", lambda v: v.tensor_tensor(
                    rows[0:1, CH:2 * CH], rows[0:1, CH:2 * CH], rows[0:1, 2 * CH:3 * CH], ALU.subtract), inc=cv)
                W("scalar", ev_gv)
                E.do("scalar", lambda s: s.activation(rows[0:1, CH:2 * CH], rows[0:1, CH:2 * CH], AF.Ln, bias=c_bias[0:1, 0:1]))
                ev_grs = E.do("scalar", lambda s: s.activation(rows[0:1, CH:2 * CH], rows[0:1, CH:2 * CH], AF.Exp, scale=-0.5), inc=cs)
                W("vector", ev_grs)
                E.do("vector", lambda v: v.tensor_tensor(rows[0:1, 2 * CH:3 * CH], rows[0:1, 0:CH], rows[0:1, CH:2 * CH], ALU.mult))
                ev_gm = E.do("vector", lambda v: v.tensor_scalar(
                    rows[0:1, 2 * CH:3 * CH], rows[0:1, 2 * CH:3 * CH], -1.0, None, ALU.mult), inc=cv)
                W("tensor", ev_gm)
                ev_br1 = E.do("tensor", lambda t: t.matmul(PS[1][0:128, 0:CH], c_onesr32[:], rows[0:1, CH:2 * CH], start=True, stop=True), inc=cm)
                ev_br2 = E.do("tensor", lambda t: t.matmul(PS[1][0:128, CH:2 * CH], c_onesr32[:], rows[0:1, 2 * CH:3 * CH], start=True, stop=True), inc=cm)
                W("vector", ev_br2)
                E.do("vector", lambda v: v.tensor_tensor(
                    g3, g3, PS[1][0:128, 0:CH].unsqueeze(1).to_broadcast([128, NDT, CH]), ALU.mult))
                ev_gn = E.do("vector", lambda v: v.tensor_tensor(
                    g3, g3, PS[1][0:128, CH:2 * CH].unsqueeze(1).to_broadcast([128, NDT, CH]), ALU.add), inc=cv)
                W("tensor", ev_gn)
                W("tensor", ev_wout)
                tcpar = (c // 2) % 2
                half = c % 2
                ev_at = None
                for ot in range(NDT):
                    pben = PS[ot % 2]
                    ev_mm = None
                    for dt in range(NDT):
                        ev_mm = E.do("tensor", lambda t, dt=dt, ot=ot, pben=pben: t.matmul(
                            pben[0:128, 0:CH], wio4[:, dt, ot, :],
                            g3[:, dt, :], start=(dt == 0), stop=(dt == NDT - 1)),
                            inc=(cm if dt == NDT - 1 else None))
                    W("vector", ev_mm)
                    ev_at = E.do("vector", lambda v, ot=ot, pben=pben: v.tensor_scalar(
                        sq3[:, ot, :], pben[0:128, 0:CH], p_bout[:, ot:ot + 1], None, ALU.add), inc=cv)
                    W("tensor", ev_at)
                W("gpsimd", ev_at)
                W("gpsimd", ev_ff_memset.get(c // 2))
                ev_fa = E.do("gpsimd", lambda g, tcpar=tcpar, half=half: g.tensor_tensor(
                    ffnacc[tcpar][:, :, half * CH:(half + 1) * CH],
                    ffnacc[tcpar][:, :, half * CH:(half + 1) * CH], sq3, ALU.add), inc=cp)
                ev_ffn_w[tcpar] = ev_fa
                st["ev_fa_prev"] = ev_fa

            def s5_seg(tc):
                tcpar = tc % 2
                for half in range(2):
                    c = tc * 2 + half
                    t0 = c * CH
                    W("sync", (st_out, st_out.n))
                    W("sync", (cs, cs.n))
                    ev_l = sp_dma(iob13, stg1_d[ib][:, t0:t0 + CH].rearrange("(a p) c -> p a c", p=128),
                                  s_stg, split=4)
                    W("vector", ev_l)
                    W("vector", (cp, cp.n))
                    ev_o = E.do("vector", lambda v, tcpar=tcpar, half=half: v.tensor_tensor(
                        iob13, iob13, ffnacc[tcpar][:, :, half * CH:(half + 1) * CH], ALU.add), inc=cv)
                    ev_s5_done[tcpar] = ev_o
                    st["ev_s5_comb"] = ev_o
                    W("sync", ev_o)
                    sp_dma(out_fm[ib][:, t0:t0 + CH].rearrange("(a p) c -> p a c", p=128),
                           iob13, st_out, split=4)
                st["ev_s5_store"] = (st_out, st_out.n)

            moe_segs = []
            for tc in range(NTC):
                moe_segs.append(lambda tc=tc: ffn_memset_seg(tc))
                for e_ in range(NE):
                    for fs in range(NFS):
                        moe_segs.extend(moe_unit_segs(tc, e_, fs))
            scan_segs = []
            for c in range(NCH):
                sl = [lambda c=c: scan_head_seg(c)]
                for n in range(NST):
                    sl.append(lambda c=c, n=n: scan_n_seg(c, n))
                sl.append(lambda c=c: scan_tail_seg(c))
                scan_segs.append(sl)

            mi = 0
            n_moe = len(moe_segs)
            for c in range(NCH):
                segs = scan_segs[c]
                end = (c + 1) * n_moe // NCH
                share = moe_segs[mi:end]
                mi = end
                k = 0
                ratio = max(1, (len(share) + len(segs) - 1) // len(segs))
                for sseg in segs:
                    sseg()
                    for _ in range(ratio):
                        if k < len(share):
                            share[k]()
                            k += 1
                while k < len(share):
                    share[k]()
                    k += 1
                if c % 2 == 1:
                    s5_seg(c // 2)

        all_ctrs = [cm, cv, cs, cp, cx, cwm, csml, cst, st_out,
                    s_stg, s_wio] + cbigE + s_x + s_h0 + s_cw + s_w13 + s_w2
        if stage != "full":
            for eng in Emit.ENGS:
                for c_ in all_ctrs:
                    W(eng, (c_, c_.n) if not c_.dma else (c_, c_.n))
        def fin(e):
            e.wait_ge(st_out.h, st_out.n)
            e.wait_ge(cst.h, cst.n)
        for eng in ("sync", "gpsimd"):
            SCHED[eng].append(("w", st_out.name, st_out.n))
            SCHED[eng].append(("w", cst.name, cst.n))
            E.ops[eng].append(fin)

        @block.sync
        def _(eng):
            for f in E.ops["sync"]:
                f(eng)

        @block.gpsimd
        def _(eng):
            for f in E.ops["gpsimd"]:
                f(eng)

        @block.tensor
        def _(eng):
            for f in E.ops["tensor"]:
                f(eng)

        @block.vector
        def _(eng):
            for f in E.ops["vector"]:
                f(eng)

        @block.scalar
        def _(eng):
            for f in E.ops["scalar"]:
                f(eng)

    return nc


_CACHE = {}


def _prep_shared(inputs):
    asc = np.ascontiguousarray
    f32 = np.float32

    def colpar(v):
        return asc(v.reshape(NDT, 128).T.astype(f32))

    d = {}
    d["w_inT"] = asc(inputs["L0_attn_w_in"].T.astype(f32))
    d["cwT"] = asc(inputs["L0_attn_conv_w"].transpose(2, 1, 0).astype(f32))
    d["w_outT"] = asc(inputs["L0_attn_w_out"].T.astype(f32))
    d["w_BT"] = asc(inputs["L0_attn_w_B"].T.astype(f32))
    d["w_CT"] = asc(inputs["L0_attn_w_C"].T.astype(f32))
    d["w_d1T"] = asc(inputs["L0_attn_w_d1"].T.astype(f32))
    d["w_d2T"] = asc(inputs["L0_attn_w_d2"].T.astype(f32))
    d["rW"] = asc(inputs["L0_ffn_router_w"].astype(f32))
    d["w1"] = asc(inputs["L0_ffn_w1"].astype(f32))
    d["w3"] = asc(inputs["L0_ffn_w3"].astype(f32))
    d["w2"] = asc(inputs["L0_ffn_w2"].astype(f32))
    d["bin_c"] = colpar(inputs["L0_attn_b_in"])
    d["cb_c"] = colpar(inputs["L0_attn_conv_b"])
    d["bd2_c"] = colpar(inputs["L0_attn_b_d2"])
    d["bout_c"] = colpar(inputs["L0_attn_b_out"])
    d["dpar_c"] = colpar(inputs["L0_attn_D"])
    d["alog_c"] = asc(inputs["L0_attn_A_log"].reshape(NDT, 128, NST)
                      .transpose(1, 0, 2).reshape(128, NDT * NST).astype(f32))
    d["bB"] = asc(inputs["L0_attn_b_B"].reshape(NST, 1).astype(f32))
    d["bC"] = asc(inputs["L0_attn_b_C"].reshape(NST, 1).astype(f32))
    d["bd1"] = asc(inputs["L0_attn_b_d1"].reshape(ND, 1).astype(f32))
    d["rb"] = asc(inputs["L0_ffn_router_b"].reshape(1, NE).astype(f32))
    d["identc"] = np.eye(128, dtype=f32)
    sel16 = np.zeros((NST, NST * 128), dtype=f32)
    for n in range(NST):
        sel16[n, n * 128:(n + 1) * 128] = 1.0
    d["sel16c"] = sel16
    sel4 = np.zeros((NE, NE * 128), dtype=f32)
    for e in range(NE):
        sel4[e, e * 128:(e + 1) * 128] = 1.0
    d["sel4c"] = sel4
    d["onesr"] = np.ones((1, 128), dtype=f32)
    d["onesc"] = np.ones((128, 1), dtype=f32)
    cb = np.zeros((128, 2), dtype=f32)
    cb[:, 0] = EPS
    cb[:, 1] = 1.0
    d["cbias"] = cb
    return d


def kernel(**inputs):
    x = np.asarray(inputs["L0_attn_x"], dtype=np.float32)
    shared = _prep_shared(inputs)
    stage = os.environ.get("KSTAGE", "full")
    if stage not in _CACHE:
        _CACHE[stage] = build_program(stage)
    nc = _CACHE[stage]
    in_maps = []
    for c in range(NCORE):
        m = dict(shared)
        xs = x[BL * c: BL * (c + 1)]
        m["x_fm"] = np.ascontiguousarray(xs.transpose(0, 2, 1))
        in_maps.append(m)
    res = run_bass_kernel_spmd(nc, in_maps, core_ids=list(range(NCORE)))
    outs = [r["out_fm"].transpose(0, 2, 1) for r in res.results]
    return np.ascontiguousarray(np.concatenate(outs, axis=0))


# version 08 (final)
# speedup vs baseline: 1.1400x; 1.1400x over previous
import os
import numpy as np
from contextlib import ExitStack

import concourse.bass as bass
import concourse.mybir as mybir
from concourse.bass_utils import run_bass_kernel_spmd

F32 = mybir.dt.float32
F16 = mybir.dt.float16
ALU = mybir.AluOpType
AF = mybir.ActivationFunctionType
AX = mybir.AxisListType

NCORE = 8
BL, L, D, NDT = 2, 2048, 1024, 8
NST, ND, NE, DFF = 16, 64, 4, 4096
CH = 256
NCH = L // CH
TC = 512
NTC = L // TC
NFS = 4
NKF = 8
NFB = NFS * NKF
EPS = 1e-5


class Ctr:
    def __init__(self, es, nc, name, dma=False):
        self.h = es.enter_context(nc.semaphore(name))
        self.n = 0
        self.name = name
        self.dma = dma


SCHED = {}


class Emit:
    ENGS = ("sync", "gpsimd", "tensor", "vector", "scalar")

    def __init__(self):
        self.ops = {e: [] for e in self.ENGS}
        self.last_wait = {e: {} for e in self.ENGS}
        SCHED.clear()
        SCHED.update({e: [] for e in self.ENGS})

    def wait(self, eng, ev):
        if ev is None:
            return
        ctr, val = ev
        if ctr.dma:
            val = ctr.n
        if val is None or val <= 0:
            return
        if self.last_wait[eng].get(ctr.name, 0) >= val:
            return
        self.last_wait[eng][ctr.name] = val
        SCHED[eng].append(("w", ctr.name, val))
        self.ops[eng].append(lambda e, c=ctr, v=val: e.wait_ge(c.h, v))

    def do(self, eng, fn, inc=None, k=1):
        if eng in ("vector", "scalar", "gpsimd"):
            self.ops[eng].append(lambda e: e.drain())
        if inc is None:
            SCHED[eng].append(("n", None, 0))
            self.ops[eng].append(fn)
            return None

        def f(e, fn=fn, c=inc, k=k):
            fn(e).then_inc(c.h, k)
        SCHED[eng].append(("i", inc.name, k))
        self.ops[eng].append(f)
        inc.n += k
        return (inc, inc.n)


def build_program(stage="full"):
    nc = bass.Bass("TRN2", target_bir_lowering=False)

    def din(name, shape, dt=F32):
        return nc.dram_tensor(name, shape, dt, kind="ExternalInput").ap()

    def dout(name, shape, dt=F32):
        return nc.dram_tensor(name, shape, dt, kind="ExternalOutput").ap()

    def dint(name, shape, dt=F32):
        return nc.dram_tensor(name, shape, dt, kind="Internal").ap()

    x_fm = din("x_fm", [BL, NCH, 128, NDT * CH])
    w_inT = din("w_inT", [D, D])
    cwT = din("cwT", [3, D, D])
    w_outT = din("w_outT", [D, D])
    w_BT_d = din("w_BT", [D, NST])
    w_CT_d = din("w_CT", [D, NST])
    w_d1T_d = din("w_d1T", [D, ND])
    w_d2T_d = din("w_d2T", [ND, D])
    rW = din("rW", [D, NE])
    w1d = din("w1", [NE, D, DFF])
    w3d = din("w3", [NE, D, DFF])
    w2d = din("w2", [NE, DFF, D])
    bin_c = din("bin_c", [128, NDT])
    cb_c = din("cb_c", [128, NDT])
    bd2_c = din("bd2_c", [128, NDT])
    bout_c = din("bout_c", [128, NDT])
    dpar_c = din("dpar_c", [128, NDT])
    alog_c = din("alog_c", [128, NDT * NST])
    bB_d = din("bB", [NST, 1])
    bC_d = din("bC", [NST, 1])
    bd1_d = din("bd1", [ND, 1])
    rb_d = din("rb", [1, NE])
    identc = din("identc", [128, 128])
    sel16c = din("sel16c", [NST, NST * 128])
    sel4c = din("sel4c", [NE, NE * 128])
    onesr = din("onesr", [1, 128])
    onesc = din("onesc", [128, 1])
    cbias = din("cbias", [128, 2])

    out_fm = dout("out_fm", [BL, NCH, 128, NDT * CH])
    stg1_d = dint("stg1_d", [BL, NCH, 128, NDT * CH])
    h0_d = dint("h0_d", [BL, NCH, 128, NDT * CH], F16)
    w_inc_d = dint("w_inc_d", [128, NDT, D], F16)
    w_outc_d = dint("w_outc_d", [128, NDT, D], F16)
    cwc_d = dint("cwc_d", [128, NDT, 3, NDT, 128], F16)
    w13c_d = dint("w13c_d", [NE, NFB, 2, NDT, 128, 128], F16)
    w2c_d = dint("w2c_d", [NE, NFS, NDT, NKF, 128, 128], F16)

    es = ExitStack()
    E = Emit()
    with es:
        def sb(name, shape, dt):
            return es.enter_context(nc.sbuf_tensor(name, shape, dt))

        def psum(name, shape, dt=F32):
            return es.enter_context(nc.psum_tensor(name, shape, dt))

        wio = sb("wio", [128, NDT * NDT * 128], F16)
        xn16 = sb("xn16", [128, NDT * L], F16)
        hp16 = sb("hp16", [128, NDT * L], F16)
        hslot = sb("hslot", [128, NDT * (L + 2)], F16)
        w13b = [sb(f"w13b{i}", [128, 2 * NDT * 128], F16) for i in range(2)]
        w2b = [sb(f"w2b{i}", [128, 2 * NKF * 128], F16) for i in range(2)]
        xfa = sb("xfa", [128, L], F32)
        xfb = sb("xfb", [128, L], F32)
        ys16 = [sb(f"ys16_{i}", [128, L], F16) for i in range(2)]
        g16 = sb("g16", [128, L], F16)
        sq16 = sb("sq16", [128, L], F16)
        iob1 = sb("iob1", [128, L], F32)
        yb = sb("yb", [128, L], F32)
        h0c = [sb(f"h0c{i}", [128, L], F16) for i in range(2)]
        gatesFM = sb("gatesFM", [NE, L], F16)
        carry = sb("carry", [128, 128], F32)
        bmT = sb("bmT", [NST, CH], F16)
        cmT = sb("cmT", [NST, CH], F16)
        d1t = sb("d1t", [ND, CH], F16)
        c_sel16 = sb("c_sel16", [NST, NST * 128], F16)
        c_sel4 = sb("c_sel4", [NE, NE * 128], F16)
        c_rw16 = sb("c_rw16", [128, NDT * NE], F16)
        c_rb16 = sb("c_rb16", [1, NE], F16)
        c_ident = sb("c_ident", [128, 128], F32)
        c_onesr16 = sb("c_onesr16", [1, 128], F16)
        c_onesr32 = sb("c_onesr32", [1, 128], F32)
        c_onesc16 = sb("c_onesc16", [128, 1], F16)
        c_onesc32 = sb("c_onesc32", [128, 1], F32)
        c_bias = sb("c_bias", [128, 2], F32)
        p_negA = sb("p_negA", [128, NDT * NST], F32)
        p_bin = sb("p_bin", [128, NDT], F32)
        p_cb = sb("p_cb", [128, NDT], F32)
        p_bd2 = sb("p_bd2", [128, NDT], F32)
        p_bout = sb("p_bout", [128, NDT], F32)
        p_D = sb("p_D", [128, NDT], F32)
        p_bB = sb("p_bB", [NST, 1], F32)
        p_bC = sb("p_bC", [NST, 1], F32)
        p_bd1 = sb("p_bd1", [ND, 1], F32)
        p_sin = sb("p_sin", [128, NDT], F32)
        p_hbias = sb("p_hbias", [128, NDT], F32)
        p_rsb = sb("p_rsb", [128, 1], F32)
        p_mrs = sb("p_mrs", [128, 1], F32)
        wB_sb = sb("wB_sb", [128, NDT * NST], F16)
        wC_sb = sb("wC_sb", [128, NDT * NST], F16)
        wd1_sb = sb("wd1_sb", [128, NDT * ND], F16)
        wd2_sb = sb("wd2_sb", [ND, D], F16)
        st_x = sb("st_x", [128, 2 * NCH], F32)
        st_c1 = sb("st_c1", [128, NDT * NCH], F32)
        st_c2 = sb("st_c2", [128, NDT * NCH], F32)
        cn_rs2 = sb("cn_rs2", [128, NDT], F32)
        cn_bi2 = sb("cn_bi2", [128, NDT], F32)
        sc_t = sb("sc_t", [1, 16], F32)
        rows = sb("rows", [1, 4 * CH], F32)
        lgb = sb("lgb", [128, 8 * NE], F32)

        PU1 = [psum("PU1a", [128, TC]), psum("PU1b", [128, TC])]
        PU3 = [psum("PU3a", [128, TC]), psum("PU3b", [128, TC])]
        PW2 = [psum("PW2a", [128, TC]), psum("PW2b", [128, TC])]
        PS = [psum("PSa", [128, TC]), psum("PSb", [128, TC])]

        def v3(t, a, c):
            return t[:].rearrange("p (a c) -> p a c", a=a)

        xn3 = v3(xn16, NDT, L)
        hp3 = v3(hp16, NDT, L)
        h3 = hslot[:].rearrange("p (a c) -> p a c", a=NDT)
        wio4 = wio[:].rearrange("p (a b o) -> p a b o", a=NDT, b=NDT)
        he_t = [hslot[:, k * TC:(k + 1) * TC] for k in range(NKF)]
        ffnacc = [hslot[:, 4096 + i * 4096: 4096 + (i + 1) * 4096].rearrange(
            "p (a c) -> p a c", a=NDT) for i in range(2)]
        gb = hslot[:, 12288:12800]
        moeT = [hslot[:, 12800 + i * TC: 12800 + (i + 1) * TC] for i in range(2)]
        u3s = [hslot[:, 13824 + i * TC: 13824 + (i + 1) * TC] for i in range(2)]
        t16 = [hslot[:, 14848 + i * TC: 14848 + (i + 1) * TC] for i in range(2)]
        abar = [xfa[:].bitcast(F16)[:, 0:L].rearrange("p (a c) -> p a c", a=NDT),
                xfa[:].bitcast(F16)[:, L:2 * L].rearrange("p (a c) -> p a c", a=NDT)]
        bx3 = xfb[:].bitcast(F16)[:, 0:L].rearrange("p (a c) -> p a c", a=NDT)
        dh3 = xfb[:].bitcast(F16)[:, L:2 * L].rearrange("p (a c) -> p a c", a=NDT)
        zb3 = bx3
        dl3 = v3(iob1, NDT, CH)
        y3 = v3(yb, NDT, CH)
        g3 = v3(g16, NDT, CH)
        sq3 = v3(sq16, NDT, CH)
        ys3 = [v3(ys16[0], NDT, CH), v3(ys16[1], NDT, CH)]
        iob13 = v3(iob1, NDT, CH)
        w13b4 = [t[:].rearrange("p (w a o) -> p w a o", w=2, a=NDT) for t in w13b]
        w2b4 = [t[:].rearrange("p (w a o) -> p w a o", w=2, a=NKF) for t in w2b]

        cm = Ctr(es, nc, "cm")
        cv = Ctr(es, nc, "cv")
        cs = Ctr(es, nc, "cs")
        cp = Ctr(es, nc, "cp")
        cx = Ctr(es, nc, "cx", dma=True)
        cwm = Ctr(es, nc, "cwm", dma=True)
        csml = Ctr(es, nc, "csml", dma=True)
        cbigE = [Ctr(es, nc, f"cbigE{e}", dma=True) for e in range(NE)]
        cst = Ctr(es, nc, "cst", dma=True)
        s_x = [Ctr(es, nc, f"s_x{i}", dma=True) for i in range(2)]
        s_h0 = [Ctr(es, nc, f"s_h0{i}", dma=True) for i in range(2)]
        s_stg = Ctr(es, nc, "s_stg", dma=True)
        s_cw = [Ctr(es, nc, f"s_cw{i}", dma=True) for i in range(2)]
        s_w13 = [Ctr(es, nc, f"s_w13{i}", dma=True) for i in range(2)]
        s_w2 = [Ctr(es, nc, f"s_w2{i}", dma=True) for i in range(2)]
        s_wio = Ctr(es, nc, "s_wio", dma=True)
        st_out = Ctr(es, nc, "st_out", dma=True)

        block = es.enter_context(nc.Block())
        W = E.wait

        def sp_dma(out, in_, ctr, split=1):
            if split == 1:
                def f(e, out=out, in_=in_, c=ctr):
                    e.dma_start(out=out, in_=in_).then_inc(c.h, 16)
                E.do("sync", f)
                SCHED["sync"][-1] = ("i", ctr.name, 16)
                ctr.n += 16
                return (ctr, ctr.n)
            step = 128 // split
            for i in range(split):
                o = out[i * step:(i + 1) * step]
                ii = in_[i * step:(i + 1) * step]

                def f(e, o=o, ii=ii, c=ctr):
                    e.dma_start(out=o, in_=ii).then_inc(c.h, 16)
                E.do("sync", f)
                SCHED["sync"][-1] = ("i", ctr.name, 16)
                ctr.n += 16
            return (ctr, ctr.n)

        def gp_dma(out, in_, ctr=None, pre=True):
            c = ctr if ctr is not None else cwm
            if pre:
                E.wait("gpsimd", (c, c.n))

            def f(e, out=out, in_=in_, c=c):
                e.dma_start(out=out, in_=in_).then_inc(c.h, 16)
            E.do("gpsimd", f)
            SCHED["gpsimd"][-1] = ("i", c.name, 16)
            c.n += 16
            return (c, c.n)

        def sync_ser(out, in_):
            E.wait("sync", (cx, cx.n))

            def f(e, out=out, in_=in_):
                e.dma_start(out=out, in_=in_).then_inc(cx.h, 16)
            E.do("sync", f)
            SCHED["sync"][-1] = ("i", cx.name, 16)
            cx.n += 16
            return (cx, cx.n)

        gp_dma(w_inc_d, w_inT.rearrange("(a p) o -> p a o", p=128), ctr=csml)
        gp_dma(w_outc_d, w_outT.rearrange("(a p) o -> p a o", p=128), ctr=csml, pre=False)
        gp_dma(c_sel16[:], sel16c)
        gp_dma(c_sel4[:], sel4c, pre=False)
        gp_dma(c_onesc16[:], onesc, pre=False)
        gp_dma(c_onesr16[:], onesr, pre=False)
        gp_dma(c_rb16[:], rb_d, pre=False)
        gp_dma(c_rw16[:].rearrange("p (a e) -> p a e", a=NDT),
               rW.rearrange("(a p) e -> p a e", p=128), pre=False)
        gp_dma(wB_sb[:].rearrange("p (a n) -> p a n", a=NDT),
               w_BT_d.rearrange("(a p) n -> p a n", p=128), pre=False)
        gp_dma(wC_sb[:].rearrange("p (a n) -> p a n", a=NDT),
               w_CT_d.rearrange("(a p) n -> p a n", p=128), pre=False)
        gp_dma(wd1_sb[:].rearrange("p (a n) -> p a n", a=NDT),
               w_d1T_d.rearrange("(a p) n -> p a n", p=128), pre=False)
        ev_pre_g = gp_dma(wd2_sb[:], w_d2T_d, pre=False)
        E.do("gpsimd", lambda g: g.memset(h3[:, :, 0:1], 0.0))
        ev_gz0 = E.do("gpsimd", lambda g: g.memset(h3[:, :, L + 1:L + 2], 0.0), inc=cp, k=1)
        cw_src = cwT.rearrange("t (a p) (b o) -> p b t a o", p=128, o=128)
        for ot_ in range(NDT):
            for t_ in range(3):
                gp_dma(cwc_d[:, ot_, t_], cw_src[:, ot_, t_], ctr=csml, pre=False)
        ev_csml = (csml, csml.n)
        for e_ in range(NE):
            w1_src = w1d[e_].rearrange("(a p) (b o) -> b a p o", p=128, o=128)
            w3_src = w3d[e_].rearrange("(a p) (b o) -> b a p o", p=128, o=128)
            for q in range(NFB):
                gp_dma(w13c_d[e_, q, 0], w1_src[q], ctr=cbigE[e_], pre=False)
                gp_dma(w13c_d[e_, q, 1], w3_src[q], ctr=cbigE[e_], pre=False)
            w2_src = w2d[e_].rearrange("(f k p) (d o) -> d f k p o", k=NKF, p=128, o=128)
            for dt_ in range(NDT):
                for f_ in range(NFS):
                    gp_dma(w2c_d[e_, f_, dt_], w2_src[dt_, f_], ctr=cbigE[e_], pre=False)
        ev_cbigE = [(cbigE[e_], cbigE[e_].n) for e_ in range(NE)]
        sync_ser(c_ident[:], identc)
        sync_ser(c_onesr32[:], onesr)
        sync_ser(c_onesc32[:], onesc)
        sync_ser(p_negA[:], alog_c)
        sync_ser(p_bin[:], bin_c)
        sync_ser(p_cb[:], cb_c)
        sync_ser(p_bd2[:], bd2_c)
        sync_ser(p_bout[:], bout_c)
        sync_ser(p_D[:], dpar_c)
        sync_ser(p_bB[:], bB_d)
        sync_ser(p_bC[:], bC_d)
        sync_ser(p_bd1[:], bd1_d)
        ev_pre_x = sync_ser(c_bias[:], cbias)
        W("scalar", ev_pre_x)
        E.do("scalar", lambda s: s.activation(p_negA[:], p_negA[:], AF.Exp, scale=1.0))
        ev_negA = E.do("scalar", lambda s: s.mul(p_negA[:], p_negA[:], -1.0), inc=cs)

        W("sync", ev_csml)
        ev_wio = sp_dma(wio[:].rearrange("p (a c) -> p a c", a=NDT),
                        w_inc_d, s_wio, split=2)
        W("tensor", ev_wio)
        W("tensor", ev_pre_x)
        W("tensor", ev_pre_g)
        for ot in range(NDT):
            ev_mm = None
            for dt in range(NDT):
                ev_mm = E.do("tensor", lambda t, dt=dt, ot=ot: t.matmul(
                    PS[0][0:128, 0:1], wio4[:, dt, ot, :], c_onesc16[:],
                    start=(dt == 0), stop=(dt == NDT - 1)),
                    inc=(cm if dt == NDT - 1 else None))
            W("scalar", ev_mm)
            ev_sc = E.do("scalar", lambda s, ot=ot: s.copy(
                p_sin[:, ot:ot + 1], PS[0][0:128, 0:1]), inc=cs)
            W("tensor", ev_sc)

        moe_e_cast_ok = set()
        nbatch = 0 if stage == "pre" else (1 if stage in ("A", "A1", "A3", "B") else BL)
        for ib in range(nbatch):
            xb = x_fm[ib]
            if ib > 0:
                bars = [(cm, cm.n), (cv, cv.n), (cs, cs.n), (cp, cp.n),
                        (cst, cst.n), (st_out, st_out.n), (cwm, cwm.n), (cx, cx.n)]
                for sl in (s_x + s_h0 + [s_stg] + s_cw + s_w13 + s_w2 + [s_wio]):
                    bars.append((sl, sl.n))
                for eng in Emit.ENGS:
                    for b in bars:
                        W(eng, b)
                ev_wio = sp_dma(wio[:].rearrange("p (a c) -> p a c", a=NDT),
                                w_inc_d, s_wio, split=2)

            if ib == 0:
                ev_gz = ev_gz0
            else:
                ev_gz = E.do("gpsimd", lambda g: g.memset(h3[:, :, 0:1], 0.0))
                ev_gz = E.do("gpsimd", lambda g: g.memset(h3[:, :, L + 1:L + 2], 0.0), inc=cp, k=1)

            ev_stg_pipe = {}
            ev_router_last = None
            ev_bst = None
            ev_bc2 = None
            ev_mmst = None
            ev_stg = None
            ev_sq = None
            ev_var_prev = None
            for c in range(NCH):
                t0 = c * CH
                par = c % 2
                xf = (xfa if par == 0 else xfb)[:].rearrange("p (a c) -> p a c", a=NDT)
                xfr = (xfa if par == 0 else xfb)
                W("sync", ev_mmst)
                W("sync", ev_stg)
                ev_x32 = sp_dma(xfr[:], xb[c], s_x[par], split=4)
                W("scalar", ev_x32)
                W("scalar", ev_sq)
                W("scalar", ev_stg)
                ev_sq = E.do("scalar", lambda s, xfr=xfr: s.activation(
                    yb[:], xfr[:], AF.Square), inc=cs)
                W("tensor", ev_x32)
                W("tensor", ev_var_prev)
                for dt in range(NDT):
                    E.do("tensor", lambda t, dt=dt, xf=xf: t.matmul(
                        PU1[0][0:1, 0:CH], c_onesc32[:], xf[:, dt, :],
                        start=(dt == 0), stop=(dt == NDT - 1)))
                W("tensor", ev_sq)
                ev_mmst = None
                for dt in range(NDT):
                    ev_mmst = E.do("tensor", lambda t, dt=dt: t.matmul(
                        PU1[0][0:1, CH:2 * CH], c_onesc32[:], y3[:, dt, :],
                        start=(dt == 0), stop=(dt == NDT - 1)),
                        inc=(cm if dt == NDT - 1 else None))
                W("vector", ev_mmst)
                W("vector", ev_bc2)
                E.do("vector", lambda v: v.tensor_scalar(rows[0:1, 0:CH], PU1[0][0:1, 0:CH], 1.0 / D, None, ALU.mult))
                E.do("vector", lambda v: v.tensor_scalar(rows[0:1, CH:2 * CH], PU1[0][0:1, CH:2 * CH], 1.0 / D, None, ALU.mult))
                E.do("vector", lambda v: v.tensor_tensor(rows[0:1, 2 * CH:3 * CH], rows[0:1, 0:CH], rows[0:1, 0:CH], ALU.mult))
                ev_var = E.do("vector", lambda v: v.tensor_tensor(
                    rows[0:1, CH:2 * CH], rows[0:1, CH:2 * CH], rows[0:1, 2 * CH:3 * CH], ALU.subtract), inc=cv)
                ev_var_prev = ev_var
                W("scalar", ev_var)
                E.do("scalar", lambda s: s.activation(rows[0:1, CH:2 * CH], rows[0:1, CH:2 * CH], AF.Ln, bias=c_bias[0:1, 0:1]))
                ev_rs = E.do("scalar", lambda s: s.activation(
                    rows[0:1, CH:2 * CH], rows[0:1, CH:2 * CH], AF.Exp, scale=-0.5), inc=cs)
                W("vector", ev_rs)
                E.do("vector", lambda v: v.tensor_tensor(rows[0:1, 2 * CH:3 * CH], rows[0:1, 0:CH], rows[0:1, CH:2 * CH], ALU.mult))
                ev_m2 = E.do("vector", lambda v: v.tensor_scalar(
                    rows[0:1, 2 * CH:3 * CH], rows[0:1, 2 * CH:3 * CH], -1.0, None, ALU.mult), inc=cv)
                ev_rs1 = E.do("vector", lambda v: v.tensor_scalar(
                    rows[0:1, 3 * CH:4 * CH], rows[0:1, CH:2 * CH], 1.0, None, ALU.add), inc=cv)
                W("tensor", ev_m2)
                W("tensor", ev_rs1)
                E.do("tensor", lambda t: t.matmul(PU3[0][0:128, 0:CH], c_onesr32[:], rows[0:1, CH:2 * CH], start=True, stop=True))
                ev_bc1 = E.do("tensor", lambda t: t.matmul(PU3[0][0:128, CH:2 * CH], c_onesr32[:], rows[0:1, 2 * CH:3 * CH], start=True, stop=True), inc=cm)
                ev_bc2 = E.do("tensor", lambda t: t.matmul(PU3[1][0:128, 0:CH], c_onesr32[:], rows[0:1, 3 * CH:4 * CH], start=True, stop=True), inc=cm)
                W("vector", ev_bc1)
                bcast_rs = PU3[0][0:128, 0:CH].unsqueeze(1).to_broadcast([128, NDT, CH])
                bcast_m2 = PU3[0][0:128, CH:2 * CH].unsqueeze(1).to_broadcast([128, NDT, CH])
                E.do("vector", lambda v, xf=xf, b=bcast_rs: v.tensor_tensor(y3, xf, b, ALU.mult))
                ev_xn = E.do("vector", lambda v, t0=t0, b=bcast_m2: v.tensor_tensor(
                    xn3[:, :, t0:t0 + CH], y3, b, ALU.add), inc=cv)
                W("vector", ev_bc2)
                bcast_rs1 = PU3[1][0:128, 0:CH].unsqueeze(1).to_broadcast([128, NDT, CH])
                W("vector", ev_stg_pipe.get(c - 1))
                E.do("vector", lambda v, xf=xf, b=bcast_rs1: v.tensor_tensor(y3, xf, b, ALU.mult))
                ev_stg = E.do("vector", lambda v, b=bcast_m2: v.tensor_tensor(iob13, y3, b, ALU.add), inc=cv)
                W("sync", ev_stg)
                ev_stg_pipe[c] = sp_dma(stg1_d[ib, c], iob1[:], cst, split=2)
                W("scalar", ev_xn)
                E.do("scalar", lambda s, c=c, t0=t0: s.activation(
                    g16[:], xn3[:, :, t0:t0 + CH], AF.Identity, accum_out=st_x[:, c:c + 1]))
                ev_bst = E.do("scalar", lambda s, c=c, t0=t0: s.activation(
                    g16[:], xn3[:, :, t0:t0 + CH], AF.Square, accum_out=st_x[:, NCH + c:NCH + c + 1]), inc=cs)
                W("tensor", ev_xn)
                W("tensor", ev_stg)
                W("vector", ev_bst)

                ev_gfm = None
                for tt in range(2):
                    base = tt * NE
                    W("tensor", ev_xn)
                    for dt in range(NDT):
                        E.do("tensor", lambda t, dt=dt, tt=tt, base=base, t0=t0: t.matmul(
                            PS[0][0:128, base:base + NE],
                            xn3[:, dt, t0 + tt * 128: t0 + (tt + 1) * 128],
                            c_rw16[:, dt * NE:(dt + 1) * NE],
                            start=(dt == 0), stop=False))
                    ev_lg = E.do("tensor", lambda t, base=base: t.matmul(
                        PS[0][0:128, base:base + NE], c_onesr16[:], c_rb16[:],
                        start=False, stop=True), inc=cm)
                    W("vector", ev_lg)
                    s0_ = lgb[:, tt * 16 + 0: tt * 16 + 4]
                    s1_ = lgb[:, tt * 16 + 4: tt * 16 + 8]
                    s2_ = lgb[:, tt * 16 + 8: tt * 16 + 9]
                    s3_ = lgb[:, tt * 16 + 9: tt * 16 + 10]
                    s4_ = lgb[:, tt * 16 + 10: tt * 16 + 11]
                    ev_cp0 = E.do("vector", lambda v, s0_=s0_, base=base: v.tensor_copy(
                        s0_, PS[0][0:128, base:base + NE]), inc=cv)
                    W("tensor", ev_cp0)
                    ev_neg = E.do("vector", lambda v, s0_=s0_, s2_=s2_: v.tensor_reduce(
                        s2_, s0_, AX.X, ALU.max, negate=True), inc=cv)
                    W("scalar", ev_neg)
                    ev_exp = E.do("scalar", lambda s, s0_=s0_, s2_=s2_: s.activation(
                        s0_, s0_, AF.Exp, bias=s2_), inc=cs)
                    W("vector", ev_exp)
                    E.do("vector", lambda v, s0_=s0_, s2_=s2_: v.tensor_reduce(s2_, s0_, AX.X, ALU.add))
                    E.do("vector", lambda v, s2_=s2_, s3_=s3_: v.reciprocal(s3_, s2_))
                    E.do("vector", lambda v, s0_=s0_, s3_=s3_: v.tensor_scalar(s0_, s0_, s3_, None, ALU.mult))
                    E.do("vector", lambda v, s0_=s0_, s2_=s2_: v.tensor_reduce(s2_, s0_, AX.X, ALU.max))
                    E.do("vector", lambda v, s0_=s0_, s1_=s1_, s2_=s2_: v.tensor_scalar(s1_, s0_, s2_, None, ALU.is_equal))
                    E.do("vector", lambda v, s0_=s0_, s1_=s1_: v.scalar_tensor_tensor(s1_, s1_, -10.0, s0_, ALU.mult, ALU.add))
                    E.do("vector", lambda v, s1_=s1_, s3_=s3_: v.tensor_reduce(s3_, s1_, AX.X, ALU.max))
                    E.do("vector", lambda v, s2_=s2_, s3_=s3_, s4_=s4_: v.tensor_tensor(s4_, s2_, s3_, ALU.add))
                    E.do("vector", lambda v, s4_=s4_: v.reciprocal(s4_, s4_))
                    E.do("vector", lambda v, s0_=s0_, s1_=s1_, s3_=s3_: v.tensor_scalar(s1_, s0_, s3_, None, ALU.is_ge))
                    E.do("vector", lambda v, s0_=s0_, s4_=s4_: v.tensor_scalar(s0_, s0_, s4_, None, ALU.mult))
                    ev_gt = E.do("vector", lambda v, s0_=s0_, s1_=s1_: v.tensor_tensor(s0_, s0_, s1_, ALU.mult), inc=cv)
                    W("tensor", ev_gt)
                    ev_tr = E.do("tensor", lambda t, s0_=s0_: t.transpose(
                        PS[1][0:NE, 128:256], s0_, c_ident[:]), inc=cm)
                    W("scalar", ev_tr)
                    ev_gfm = E.do("scalar", lambda s, tt=tt, t0=t0: s.copy(
                        gatesFM[0:NE, t0 + tt * 128: t0 + (tt + 1) * 128],
                        PS[1][0:NE, 128:256]), inc=cs)
                    W("tensor", ev_gfm)
                    W("vector", ev_gfm)
                ev_router_last = ev_gfm

            if stage == "A1":
                continue
            ev_r1 = E.do("vector", lambda v: v.tensor_reduce(st_x[:, 0:1], st_x[:, 0:NCH], AX.X, ALU.add), inc=cv)
            ev_r2 = E.do("vector", lambda v: v.tensor_reduce(st_x[:, NCH:NCH + 1], st_x[:, NCH:2 * NCH], AX.X, ALU.add), inc=cv)
            W("tensor", ev_r1)
            W("tensor", ev_r2)
            ev_s1 = E.do("tensor", lambda t: t.matmul(PU1[0][0:1, 0:1], st_x[:, 0:1], c_onesc32[:], start=True, stop=True), inc=cm)
            ev_s2 = E.do("tensor", lambda t: t.matmul(PU1[0][0:1, 1:2], st_x[:, NCH:NCH + 1], c_onesc32[:], start=True, stop=True), inc=cm)
            W("vector", ev_s2)
            scv = 1.0 / (L * D)
            E.do("vector", lambda v: v.tensor_scalar(sc_t[0:1, 0:1], PU1[0][0:1, 0:1], scv, None, ALU.mult))
            E.do("vector", lambda v: v.tensor_scalar(sc_t[0:1, 1:2], PU1[0][0:1, 1:2], scv, None, ALU.mult))
            E.do("vector", lambda v: v.tensor_tensor(sc_t[0:1, 2:3], sc_t[0:1, 0:1], sc_t[0:1, 0:1], ALU.mult))
            ev_v = E.do("vector", lambda v: v.tensor_tensor(sc_t[0:1, 1:2], sc_t[0:1, 1:2], sc_t[0:1, 2:3], ALU.subtract), inc=cv)
            W("scalar", ev_v)
            E.do("scalar", lambda s: s.activation(sc_t[0:1, 1:2], sc_t[0:1, 1:2], AF.Ln, bias=c_bias[0:1, 0:1]))
            ev_rsb = E.do("scalar", lambda s: s.activation(sc_t[0:1, 1:2], sc_t[0:1, 1:2], AF.Exp, scale=-0.5), inc=cs)
            W("vector", ev_rsb)
            E.do("vector", lambda v: v.tensor_tensor(sc_t[0:1, 3:4], sc_t[0:1, 0:1], sc_t[0:1, 1:2], ALU.mult))
            ev_mrs = E.do("vector", lambda v: v.tensor_scalar(sc_t[0:1, 3:4], sc_t[0:1, 3:4], -1.0, None, ALU.mult), inc=cv)
            W("tensor", ev_mrs)
            ev_b1 = E.do("tensor", lambda t: t.matmul(PU1[0][0:128, 2:3], c_onesr32[:], sc_t[0:1, 1:2], start=True, stop=True), inc=cm)
            ev_b2 = E.do("tensor", lambda t: t.matmul(PU1[0][0:128, 3:4], c_onesr32[:], sc_t[0:1, 3:4], start=True, stop=True), inc=cm)
            W("scalar", ev_b1)
            W("scalar", ev_b2)
            E.do("scalar", lambda s: s.copy(p_rsb[:], PU1[0][0:128, 2:3]))
            ev_mrsc = E.do("scalar", lambda s: s.copy(p_mrs[:], PU1[0][0:128, 3:4]), inc=cs)
            W("vector", ev_mrsc)
            ev_hb = E.do("vector", lambda v: v.scalar_tensor_tensor(
                p_hbias[:], p_sin[:], p_mrs[:, 0:1], p_bin[:], ALU.mult, ALU.add), inc=cv)
            W("tensor", ev_hb)
            W("scalar", ev_hb)

            W("tensor", ev_wio)
            W("tensor", ev_gz)
            ev_hev = None
            for c in range(NCH):
                t0 = c * CH
                for ot in range(NDT):
                    pben = PW2[(c * NDT + ot) % 2]
                    ev_mm = None
                    for dt in range(NDT):
                        ev_mm = E.do("tensor", lambda t, dt=dt, ot=ot, pben=pben, t0=t0: t.matmul(
                            pben[0:128, 0:CH], wio4[:, dt, ot, :],
                            xn3[:, dt, t0:t0 + CH], start=(dt == 0), stop=(dt == NDT - 1)),
                            inc=(cm if dt == NDT - 1 else None))
                    W("scalar", ev_mm)
                    ev_hev = E.do("scalar", lambda s, ot=ot, pben=pben, t0=t0: s.activation(
                        h3[:, ot, t0 + 1: t0 + 1 + CH], pben[0:128, 0:CH], AF.Identity,
                        scale=p_rsb[:, 0:1], bias=p_hbias[:, ot:ot + 1]), inc=cs)
                    W("tensor", ev_hev)
                W("sync", ev_hev)
                sp_dma(h0_d[ib, c].rearrange("p (a c) -> p a c", a=NDT),
                       h3[:, :, t0 + 1:t0 + 1 + CH], cst, split=2)

            if stage == "A3":
                continue
            W("sync", ev_hev)
            ev_wout = sp_dma(wio[:].rearrange("p (a c) -> p a c", a=NDT),
                             w_outc_d, s_wio, split=2)

            ev_st2 = None
            conv_mm_ev = {}
            for ot in range(NDT):
                slot = ot % 2
                W("sync", conv_mm_ev.get(slot))
                ev_cwA = sp_dma(w13b[slot][:].rearrange("p (w a o) -> p w a o", w=2, a=NDT)[:, 0],
                                cwc_d[:, ot, 0], s_cw[slot])
                ev_cwA = sp_dma(w13b[slot][:].rearrange("p (w a o) -> p w a o", w=2, a=NDT)[:, 1],
                                cwc_d[:, ot, 1], s_cw[slot])
                ev_cwB = sp_dma(w2b[slot][:, 0:NDT * 128].rearrange("p (a o) -> p a o", a=NDT),
                                cwc_d[:, ot, 2], s_cw[slot])
                W("tensor", ev_cwA)
                W("tensor", ev_cwB)
                for c in range(NCH):
                    t0 = c * CH
                    pben = PW2[c % 2]
                    ev_mm = None
                    k = 0
                    for tau in range(3):
                        for it in range(NDT):
                            if tau < 2:
                                lhs = w13b4[slot][:, tau, it, :]
                            else:
                                lhs = w2b4[slot][:, 0, it, :]
                            ev_mm = E.do("tensor", lambda t, lhs=lhs, tau=tau, it=it, k=k, pben=pben, t0=t0: t.matmul(
                                pben[0:128, 0:CH], lhs,
                                h3[:, it, t0 + tau: t0 + tau + CH],
                                start=(k == 0), stop=(k == 23)),
                                inc=(cm if k == 23 else None))
                            k += 1
                    conv_mm_ev[slot] = ev_mm
                    W("scalar", ev_mm)
                    col = ot * NCH + c
                    E.do("scalar", lambda s, pben=pben, ot=ot, t0=t0, col=col: s.activation(
                        hp3[:, ot, t0:t0 + CH], pben[0:128, 0:CH], AF.Identity,
                        bias=p_cb[:, ot:ot + 1], accum_out=st_c1[:, col:col + 1]))
                    ev_st2 = E.do("scalar", lambda s, pben=pben, ot=ot, col=col: s.activation(
                        g16[:, 0:CH], pben[0:128, 0:CH], AF.Square,
                        bias=p_cb[:, ot:ot + 1], accum_out=st_c2[:, col:col + 1]), inc=cs)
                    W("tensor", ev_st2)
            W("vector", ev_st2)
            E.do("vector", lambda v: v.tensor_reduce(
                cn_rs2[:], st_c1[:].rearrange("p (a c) -> p a c", a=NDT), AX.X, ALU.add))
            ev_c2 = E.do("vector", lambda v: v.tensor_reduce(
                cn_bi2[:], st_c2[:].rearrange("p (a c) -> p a c", a=NDT), AX.X, ALU.add), inc=cv)
            E.do("vector", lambda v: v.tensor_scalar(cn_rs2[:], cn_rs2[:], 1.0 / L, None, ALU.mult))
            E.do("vector", lambda v: v.tensor_scalar(cn_bi2[:], cn_bi2[:], 1.0 / L, None, ALU.mult))
            E.do("vector", lambda v: v.tensor_tensor(st_c1[:, 0:NDT], cn_rs2[:], cn_rs2[:], ALU.mult))
            ev_cv_ = E.do("vector", lambda v: v.tensor_tensor(cn_bi2[:], cn_bi2[:], st_c1[:, 0:NDT], ALU.subtract), inc=cv)
            W("scalar", ev_cv_)
            E.do("scalar", lambda s: s.activation(cn_bi2[:], cn_bi2[:], AF.Ln, bias=c_bias[:, 0:1]))
            ev_crs = E.do("scalar", lambda s: s.activation(cn_bi2[:], cn_bi2[:], AF.Exp, scale=-0.5), inc=cs)
            W("vector", ev_crs)
            E.do("vector", lambda v: v.tensor_tensor(st_c1[:, 0:NDT], cn_rs2[:], cn_bi2[:], ALU.mult))
            E.do("vector", lambda v: v.tensor_scalar(cn_rs2[:], cn_bi2[:], 0.5, None, ALU.mult))
            ev_convfin = E.do("vector", lambda v: v.tensor_scalar(cn_bi2[:], st_c1[:, 0:NDT], -0.5, None, ALU.mult), inc=cv)

            if stage in ("A",):
                continue
            W("sync", (cst, cst.n))
            W("gpsimd", (cst, cst.n))
            W("scalar", (cst, cst.n))
            W("vector", (cst, cst.n))
            ev_h0l = {}
            ev_h0l[0] = sp_dma(h0c[0][:], h0_d[ib, 0], s_h0[0], split=2)

            ev_conv_last_mm = (cm, cm.n)
            evB_cs = (cs, cs.n)
            evB_cv = (cv, cv.n)
            moe_started = [False]
            ev_he = {}
            ev_tm = {}
            ev_ffn_t = {}
            ev_ffn_w = {0: None, 1: None}
            ev_s5_done = {0: None, 1: None}
            ev_ff_memset = {}
            ev_u_sl = {0: None, 1: None}
            ev_he_sl = {0: None, 1: None}
            ev_w13_mm = {0: conv_mm_ev.get(0), 1: conv_mm_ev.get(1)}
            ev_w2_mm = {0: None, 1: None,
                        ("sl", 0): conv_mm_ev.get(0), ("sl", 1): conv_mm_ev.get(1)}
            ev_gbev = [None]

            def moe_gb_seg(tc, e):
                W("tensor", ev_router_last)
                W("tensor", ev_conv_last_mm)
                W("tensor", evB_cs)
                W("tensor", evB_cv)
                W("tensor", ev_tm.get(0))
                ev_g1 = E.do("tensor", lambda t, tc=tc, e=e: t.matmul(
                    PW2[0][0:128, 0:TC], c_sel4[0:NE, e * 128:(e + 1) * 128],
                    gatesFM[0:NE, tc * TC:(tc + 1) * TC], start=True, stop=True), inc=cm)
                W("scalar", ev_g1)
                W("scalar", ev_tm.get(0))
                W("scalar", ev_tm.get(1))
                ev_gbev[0] = E.do("scalar", lambda s: s.copy(gb, PW2[0][0:128, 0:TC]), inc=cs)
                W("tensor", ev_gbev[0])

            def moe_kf_seg(tc, e, fs, kf):
                slot = kf % 2
                par = kf % 2
                fb = fs * NKF + kf
                W("sync", ev_w13_mm[slot])
                if e not in moe_e_cast_ok:
                    W("sync", ev_cbigE[e])
                    moe_e_cast_ok.add(e)
                ev_d = sp_dma(w13b[slot][:].rearrange("p (w a o) -> p w a o", w=2, a=NDT),
                              w13c_d[e, fb].rearrange("w a p o -> p w a o"), s_w13[slot], split=4)
                W("tensor", ev_d)
                if not moe_started[0]:
                    W("tensor", ev_conv_last_mm)
                    W("tensor", evB_cs)
                    W("tensor", evB_cv)
                    moe_started[0] = True
                W("tensor", ev_u_sl[par])
                ev_mm1 = None
                for dt in range(NDT):
                    ev_mm1 = E.do("tensor", lambda t, dt=dt, slot=slot, par=par, tc=tc: t.matmul(
                        PU1[par][0:128, 0:TC], w13b4[slot][:, 0, dt, :],
                        xn3[:, dt, tc * TC:(tc + 1) * TC],
                        start=(dt == 0), stop=(dt == NDT - 1)),
                        inc=(cm if dt == NDT - 1 else None))
                ev_mm3 = None
                for dt in range(NDT):
                    ev_mm3 = E.do("tensor", lambda t, dt=dt, slot=slot, par=par, tc=tc: t.matmul(
                        PU3[par][0:128, 0:TC], w13b4[slot][:, 1, dt, :],
                        xn3[:, dt, tc * TC:(tc + 1) * TC],
                        start=(dt == 0), stop=(dt == NDT - 1)),
                        inc=(cm if dt == NDT - 1 else None))
                ev_w13_mm[slot] = ev_mm3
                W("scalar", ev_mm1)
                W("scalar", ev_he_sl[par])
                E.do("scalar", lambda s, par=par: s.activation(
                    moeT[par], PU1[par][0:128, 0:TC], AF.Silu))
                W("scalar", ev_mm3)
                ev_u = E.do("scalar", lambda s, par=par: s.copy(
                    u3s[par], PU3[par][0:128, 0:TC]), inc=cs)
                ev_u_sl[par] = ev_u
                W("gpsimd", ev_u)
                W("gpsimd", ev_w2_mm.get("last"))
                W("gpsimd", ev_conv_last_mm)
                ev_h = E.do("gpsimd", lambda g, par=par, kf=kf: g.tensor_tensor(
                    he_t[kf], moeT[par], u3s[par], ALU.mult), inc=cp)
                ev_he[kf] = ev_h
                ev_he_sl[par] = ev_h

            def moe_dt_seg(tc, e, fs, dt):
                slot = (dt // 2) % 2
                bank = dt % 2
                tcpar = tc % 2
                if dt % 2 == 0:
                    W("sync", ev_w2_mm.get(("sl", slot)))
                    ev_d = sp_dma(w2b[slot][:].rearrange("p (w a o) -> p w a o", w=2, a=NKF),
                                  w2c_d[e, fs, dt:dt + 2].rearrange("d k p o -> p d k o"),
                                  s_w2[slot], split=2)
                    ev_w2_mm[("dma", slot)] = ev_d
                if dt == 0:
                    for kf in range(NKF):
                        W("tensor", ev_he[kf])
                W("tensor", ev_w2_mm[("dma", slot)])
                W("tensor", ev_tm.get(bank))
                W("tensor", ev_gbev[0])
                ev_mm = None
                for kf in range(NKF):
                    ev_mm = E.do("tensor", lambda t, kf=kf, slot=slot, bank=bank, dt=dt: t.matmul(
                        PW2[bank][0:128, 0:TC], w2b4[slot][:, dt % 2, kf, :], he_t[kf],
                        start=(kf == 0), stop=(kf == NKF - 1)),
                        inc=(cm if kf == NKF - 1 else None))
                ev_w2_mm[("sl", slot)] = ev_mm
                ev_w2_mm["last"] = ev_mm
                W("vector", ev_mm)
                W("vector", ev_gbev[0])
                W("vector", ev_ffn_t.get(bank))
                ev_t = E.do("vector", lambda v, bank=bank: v.tensor_tensor(
                    t16[bank], PW2[bank][0:128, 0:TC], gb, ALU.mult), inc=cv)
                ev_tm[bank] = ev_t
                W("gpsimd", ev_t)
                W("gpsimd", ev_ff_memset.get(tc))
                ev_f = E.do("gpsimd", lambda g, bank=bank, dt=dt, tcpar=tcpar: g.tensor_tensor(
                    ffnacc[tcpar][:, dt, :], ffnacc[tcpar][:, dt, :], t16[bank], ALU.add), inc=cp)
                ev_ffn_t[bank] = ev_f
                ev_ffn_w[tcpar] = ev_f

            def moe_unit_segs(tc, e, fs):
                segs = []
                if fs == 0:
                    segs.append(lambda tc=tc, e=e: moe_gb_seg(tc, e))
                for kf in range(NKF):
                    segs.append(lambda tc=tc, e=e, fs=fs, kf=kf: moe_kf_seg(tc, e, fs, kf))
                for dt in range(NDT):
                    segs.append(lambda tc=tc, e=e, fs=fs, dt=dt: moe_dt_seg(tc, e, fs, dt))
                return segs

            def ffn_memset_seg(tc):
                tcpar = tc % 2
                W("gpsimd", ev_s5_done[tcpar])
                W("gpsimd", ev_conv_last_mm)
                ev = E.do("gpsimd", lambda g, tcpar=tcpar: g.memset(
                    ffnacc[tcpar][:, :, :], 0.0), inc=cp)
                ev_ff_memset[tc] = ev

            st = {}

            def scan_head_seg(c):
                t0 = c * CH
                if c + 1 < NCH:
                    W("sync", st.get(("h0_free", (c + 1) % 2)))
                    ev_h0l[c + 1] = sp_dma(
                        h0c[(c + 1) % 2][:], h0_d[ib, c + 1], s_h0[(c + 1) % 2], split=2)
                W("vector", ev_convfin)
                ev_z = None
                for ft in range(NDT):
                    ev_z = E.do("vector", lambda v, ft=ft, t0=t0: v.tensor_scalar(
                        zb3[:, ft, :], hp3[:, ft, t0:t0 + CH],
                        cn_rs2[:, ft:ft + 1], cn_bi2[:, ft:ft + 1], ALU.mult, ALU.add),
                        inc=(cv if ft == NDT - 1 else None))
                W("scalar", ev_z)
                W("scalar", st.get("ev_abar_free"))
                ev_t = E.do("scalar", lambda s: s.activation(
                    abar[0][:, :, :], zb3, AF.Tanh), inc=cs)
                W("vector", ev_t)
                ev_hp = E.do("vector", lambda v, t0=t0: v.scalar_tensor_tensor(
                    hp3[:, :, t0:t0 + CH], abar[0][:, :, :], 1.0, zb3, ALU.add, ALU.mult), inc=cv)
                st["ev_hp"] = ev_hp
                st["yadd_prev"] = st.get("yadd_ev", {})
                W("tensor", ev_hp)
                ev_mmB = None
                for dt in range(NDT):
                    ev_mmB = E.do("tensor", lambda t, dt=dt, t0=t0: t.matmul(
                        PS[0][0:NST, 0:CH], wB_sb[:, dt * NST:(dt + 1) * NST],
                        hp3[:, dt, t0:t0 + CH], start=(dt == 0), stop=(dt == NDT - 1)),
                        inc=(cm if dt == NDT - 1 else None))
                ev_mmC = None
                for dt in range(NDT):
                    ev_mmC = E.do("tensor", lambda t, dt=dt, t0=t0: t.matmul(
                        PS[1][0:NST, 0:CH], wC_sb[:, dt * NST:(dt + 1) * NST],
                        hp3[:, dt, t0:t0 + CH], start=(dt == 0), stop=(dt == NDT - 1)),
                        inc=(cm if dt == NDT - 1 else None))
                W("scalar", ev_mmB)
                ev_bm = E.do("scalar", lambda s: s.activation(bmT[:], PS[0][0:NST, 0:CH], AF.Identity, bias=p_bB[:, 0:1]), inc=cs)
                W("scalar", ev_mmC)
                ev_cm2 = E.do("scalar", lambda s: s.activation(cmT[:], PS[1][0:NST, 0:CH], AF.Identity, bias=p_bC[:, 0:1]), inc=cs)
                W("tensor", ev_bm)
                ev_mmd1 = None
                for dt in range(NDT):
                    ev_mmd1 = E.do("tensor", lambda t, dt=dt, t0=t0: t.matmul(
                        PS[0][0:ND, 0:CH], wd1_sb[:, dt * ND:(dt + 1) * ND],
                        hp3[:, dt, t0:t0 + CH], start=(dt == 0), stop=(dt == NDT - 1)),
                        inc=(cm if dt == NDT - 1 else None))
                W("scalar", ev_mmd1)
                ev_d1 = E.do("scalar", lambda s: s.activation(d1t[:], PS[0][0:ND, 0:CH], AF.Identity, bias=p_bd1[:, 0:1]), inc=cs)
                W("tensor", ev_d1)
                W("tensor", ev_cm2)
                W("scalar", st.get("ev_g_prev"))
                W("scalar", st.get("ev_s5_comb"))
                W("scalar", st.get("ev_s5_store"))
                ev_dl = None
                for ot in range(NDT):
                    pben = PS[ot % 2]
                    ev_d2 = E.do("tensor", lambda t, ot=ot, pben=pben: t.matmul(
                        pben[0:128, 256:512], wd2_sb[:, ot * 128:(ot + 1) * 128], d1t[:],
                        start=True, stop=True), inc=cm)
                    W("scalar", ev_d2)
                    E.do("scalar", lambda s, pben=pben, ot=ot: s.activation(
                        yb[:, 0:CH], pben[0:128, 256:512], AF.Exp, bias=p_bd2[:, ot:ot + 1]))
                    ev_dl = E.do("scalar", lambda s, ot=ot: s.activation(
                        dl3[:, ot, :], yb[:, 0:CH], AF.Ln, bias=c_bias[:, 1:2]), inc=cs)
                    W("tensor", ev_dl)
                st["ev_dl"] = ev_dl
                W("vector", ev_dl)
                ev_dh = E.do("vector", lambda v, t0=t0: v.tensor_tensor(
                    dh3, dl3, hp3[:, :, t0:t0 + CH], ALU.mult), inc=cv)
                st["ev_dh"] = ev_dh
                st["rep_ev"] = {}
                st["yprod_ev"] = {}
                st["yadd_ev"] = {}

            def scan_n_seg(c, n):
                t0 = c * CH
                par = n % 2
                pb = PS[par]
                W("tensor", st["yprod_ev"].get(n - 2))
                if n <= 1:
                    W("tensor", st["ev_dl"])
                ev_rB = E.do("tensor", lambda t, n=n, pb=pb: t.matmul(
                    pb[0:128, 0:CH], c_sel16[0:NST, n * 128:(n + 1) * 128], bmT[:],
                    start=True, stop=True), inc=cm)
                ev_rC = E.do("tensor", lambda t, n=n, pb=pb: t.matmul(
                    pb[0:128, CH:2 * CH], c_sel16[0:NST, n * 128:(n + 1) * 128], cmT[:],
                    start=True, stop=True), inc=cm)
                W("scalar", st["yprod_ev"].get(n - 2))
                if n <= 1:
                    W("scalar", st["ev_hp"])
                    W("scalar", st.get("ev_abar_free"))
                ev_ab = None
                for ft in range(NDT):
                    ev_ab = E.do("scalar", lambda s, ft=ft, n=n, par=par: s.activation(
                        abar[par][:, ft, :], dl3[:, ft, :], AF.Exp,
                        scale=p_negA[:, ft * NST + n: ft * NST + n + 1]),
                        inc=(cs if ft == NDT - 1 else None))
                W("vector", ev_rB)
                W("vector", st["ev_dh"])
                ev_bx = E.do("vector", lambda v, pb=pb: v.tensor_tensor(
                    bx3, dh3, pb[0:128, 0:CH].unsqueeze(1).to_broadcast([128, NDT, CH]),
                    ALU.mult), inc=cv)
                W("vector", ev_ab)
                ev_sc2 = None
                for ft in range(NDT):
                    col = n * NDT + ft
                    if c == 0:
                        ev_sc2 = E.do("vector", lambda v, ft=ft, par=par: v.tensor_tensor_scan(
                            abar[par][:, ft, :], abar[par][:, ft, :], bx3[:, ft, :],
                            0.0, ALU.mult, ALU.add), inc=(cv if ft == NDT - 1 else None))
                    else:
                        ev_sc2 = E.do("vector", lambda v, ft=ft, par=par, col=col: v.tensor_tensor_scan(
                            abar[par][:, ft, :], abar[par][:, ft, :], bx3[:, ft, :],
                            carry[:, col:col + 1], ALU.mult, ALU.add),
                            inc=(cv if ft == NDT - 1 else None))
                if c < NCH - 1:
                    E.do("vector", lambda v, n=n, par=par: v.tensor_copy(
                        carry[:, n * NDT:(n + 1) * NDT],
                        abar[par][:, :, CH - 1:CH].rearrange("p a c -> p (a c)")))
                prev_ya = st["yadd_ev"].get(n - 2)
                if prev_ya is None and n < 2:
                    prev_ya = st.get("yadd_prev", {}).get(NST - 2 + n)
                W("vector", prev_ya)
                W("vector", ev_rC)
                ev_yp = E.do("vector", lambda v, pb=pb, par=par: v.tensor_tensor(
                    ys3[par], abar[par][:, :, :],
                    pb[0:128, CH:2 * CH].unsqueeze(1).to_broadcast([128, NDT, CH]),
                    ALU.mult), inc=cv)
                st["yprod_ev"][n] = ev_yp
                W("gpsimd", ev_yp)
                if n == 0:
                    ev_ya = E.do("gpsimd", lambda g, par=par: g.tensor_copy(
                        y3[:, :, :], ys3[par]), inc=cp)
                else:
                    ev_ya = E.do("gpsimd", lambda g, par=par: g.tensor_tensor(
                        y3[:, :, :], y3[:, :, :], ys3[par], ALU.add), inc=cp)
                st["yadd_ev"][n] = ev_ya

            def scan_tail_seg(c):
                t0 = c * CH
                cpar = c % 2
                W("vector", st["yadd_ev"][NST - 1])
                ev_yd = None
                for ft in range(NDT):
                    ev_yd = E.do("vector", lambda v, ft=ft, t0=t0: v.scalar_tensor_tensor(
                        y3[:, ft, :], hp3[:, ft, t0:t0 + CH], p_D[:, ft:ft + 1],
                        y3[:, ft, :], ALU.mult, ALU.add),
                        inc=(cv if ft == NDT - 1 else None))
                W("scalar", ev_h0l[c])
                W("scalar", st["yprod_ev"].get(NST - 2))
                ev_t2 = E.do("scalar", lambda s, cpar=cpar: s.activation(
                    abar[0][:, :, :], v3(h0c[cpar], NDT, CH), AF.Tanh, scale=0.5), inc=cs)
                W("vector", ev_t2)
                ev_s0 = E.do("vector", lambda v, cpar=cpar: v.scalar_tensor_tensor(
                    abar[1][:, :, :], abar[0][:, :, :], 1.0, v3(h0c[cpar], NDT, CH),
                    ALU.add, ALU.mult), inc=cv)
                st[("h0_free", cpar)] = ev_s0
                ev_g = E.do("vector", lambda v: v.scalar_tensor_tensor(
                    g3, abar[1][:, :, :], 0.5, y3[:, :, :], ALU.mult, ALU.mult), inc=cv)
                st["ev_g_prev"] = ev_g
                st["ev_abar_free"] = ev_g
                W("scalar", ev_g)
                W("scalar", st.get("ev_fa_prev"))
                ev_sq2 = E.do("scalar", lambda s: s.activation(sq3, g3, AF.Square), inc=cs)
                W("tensor", st["yprod_ev"][NST - 1])
                W("tensor", ev_g)
                for dt in range(NDT):
                    E.do("tensor", lambda t, dt=dt: t.matmul(
                        PS[0][0:1, 0:CH], c_onesc16[:], g3[:, dt, :],
                        start=(dt == 0), stop=(dt == NDT - 1)))
                W("tensor", ev_sq2)
                ev_mg = None
                for dt in range(NDT):
                    ev_mg = E.do("tensor", lambda t, dt=dt: t.matmul(
                        PS[0][0:1, CH:2 * CH], c_onesc16[:], sq3[:, dt, :],
                        start=(dt == 0), stop=(dt == NDT - 1)),
                        inc=(cm if dt == NDT - 1 else None))
                W("vector", ev_mg)
                E.do("vector", lambda v: v.tensor_scalar(rows[0:1, 0:CH], PS[0][0:1, 0:CH], 1.0 / D, None, ALU.mult))
                E.do("vector", lambda v: v.tensor_scalar(rows[0:1, CH:2 * CH], PS[0][0:1, CH:2 * CH], 1.0 / D, None, ALU.mult))
                E.do("vector", lambda v: v.tensor_tensor(rows[0:1, 2 * CH:3 * CH], rows[0:1, 0:CH], rows[0:1, 0:CH], ALU.mult))
                ev_gv = E.do("vector", lambda v: v.tensor_tensor(
                    rows[0:1, CH:2 * CH], rows[0:1, CH:2 * CH], rows[0:1, 2 * CH:3 * CH], ALU.subtract), inc=cv)
                W("scalar", ev_gv)
                E.do("scalar", lambda s: s.activation(rows[0:1, CH:2 * CH], rows[0:1, CH:2 * CH], AF.Ln, bias=c_bias[0:1, 0:1]))
                ev_grs = E.do("scalar", lambda s: s.activation(rows[0:1, CH:2 * CH], rows[0:1, CH:2 * CH], AF.Exp, scale=-0.5), inc=cs)
                W("vector", ev_grs)
                E.do("vector", lambda v: v.tensor_tensor(rows[0:1, 2 * CH:3 * CH], rows[0:1, 0:CH], rows[0:1, CH:2 * CH], ALU.mult))
                ev_gm = E.do("vector", lambda v: v.tensor_scalar(
                    rows[0:1, 2 * CH:3 * CH], rows[0:1, 2 * CH:3 * CH], -1.0, None, ALU.mult), inc=cv)
                W("tensor", ev_gm)
                ev_br1 = E.do("tensor", lambda t: t.matmul(PS[1][0:128, 0:CH], c_onesr32[:], rows[0:1, CH:2 * CH], start=True, stop=True), inc=cm)
                ev_br2 = E.do("tensor", lambda t: t.matmul(PS[1][0:128, CH:2 * CH], c_onesr32[:], rows[0:1, 2 * CH:3 * CH], start=True, stop=True), inc=cm)
                W("vector", ev_br2)
                E.do("vector", lambda v: v.tensor_tensor(
                    g3, g3, PS[1][0:128, 0:CH].unsqueeze(1).to_broadcast([128, NDT, CH]), ALU.mult))
                ev_gn = E.do("vector", lambda v: v.tensor_tensor(
                    g3, g3, PS[1][0:128, CH:2 * CH].unsqueeze(1).to_broadcast([128, NDT, CH]), ALU.add), inc=cv)
                W("tensor", ev_gn)
                W("tensor", ev_wout)
                tcpar = (c // 2) % 2
                half = c % 2
                ev_at = None
                for ot in range(NDT):
                    pben = PS[ot % 2]
                    ev_mm = None
                    for dt in range(NDT):
                        ev_mm = E.do("tensor", lambda t, dt=dt, ot=ot, pben=pben: t.matmul(
                            pben[0:128, 0:CH], wio4[:, dt, ot, :],
                            g3[:, dt, :], start=(dt == 0), stop=(dt == NDT - 1)),
                            inc=(cm if dt == NDT - 1 else None))
                    W("vector", ev_mm)
                    ev_at = E.do("vector", lambda v, ot=ot, pben=pben: v.tensor_scalar(
                        sq3[:, ot, :], pben[0:128, 0:CH], p_bout[:, ot:ot + 1], None, ALU.add), inc=cv)
                    W("tensor", ev_at)
                W("gpsimd", ev_at)
                W("gpsimd", ev_ff_memset.get(c // 2))
                ev_fa = E.do("gpsimd", lambda g, tcpar=tcpar, half=half: g.tensor_tensor(
                    ffnacc[tcpar][:, :, half * CH:(half + 1) * CH],
                    ffnacc[tcpar][:, :, half * CH:(half + 1) * CH], sq3, ALU.add), inc=cp)
                ev_ffn_w[tcpar] = ev_fa
                st["ev_fa_prev"] = ev_fa

            def s5_seg(tc):
                tcpar = tc % 2
                for half in range(2):
                    c = tc * 2 + half
                    t0 = c * CH
                    W("sync", (st_out, st_out.n))
                    W("sync", (cs, cs.n))
                    ev_l = sp_dma(iob1[:], stg1_d[ib, c], s_stg, split=4)
                    W("vector", ev_l)
                    W("vector", (cp, cp.n))
                    ev_o = E.do("vector", lambda v, tcpar=tcpar, half=half: v.tensor_tensor(
                        iob13, iob13, ffnacc[tcpar][:, :, half * CH:(half + 1) * CH], ALU.add), inc=cv)
                    ev_s5_done[tcpar] = ev_o
                    st["ev_s5_comb"] = ev_o
                    W("sync", ev_o)
                    sp_dma(out_fm[ib, c], iob1[:], st_out, split=4)
                st["ev_s5_store"] = (st_out, st_out.n)

            moe_segs = []
            for tc in range(NTC):
                moe_segs.append(lambda tc=tc: ffn_memset_seg(tc))
                for e_ in range(NE):
                    for fs in range(NFS):
                        moe_segs.extend(moe_unit_segs(tc, e_, fs))
            scan_segs = []
            for c in range(NCH):
                sl = [lambda c=c: scan_head_seg(c)]
                for n in range(NST):
                    sl.append(lambda c=c, n=n: scan_n_seg(c, n))
                sl.append(lambda c=c: scan_tail_seg(c))
                scan_segs.append(sl)

            mi = 0
            n_moe = len(moe_segs)
            for c in range(NCH):
                segs = scan_segs[c]
                end = (c + 1) * n_moe // NCH
                share = moe_segs[mi:end]
                mi = end
                k = 0
                ratio = max(1, (len(share) + len(segs) - 1) // len(segs))
                for sseg in segs:
                    sseg()
                    for _ in range(ratio):
                        if k < len(share):
                            share[k]()
                            k += 1
                while k < len(share):
                    share[k]()
                    k += 1
                if c % 2 == 1:
                    s5_seg(c // 2)

        all_ctrs = [cm, cv, cs, cp, cx, cwm, csml, cst, st_out,
                    s_stg, s_wio] + cbigE + s_x + s_h0 + s_cw + s_w13 + s_w2
        if stage != "full":
            for eng in Emit.ENGS:
                for c_ in all_ctrs:
                    W(eng, (c_, c_.n) if not c_.dma else (c_, c_.n))
        def fin(e):
            e.wait_ge(st_out.h, st_out.n)
            e.wait_ge(cst.h, cst.n)
        for eng in ("sync", "gpsimd"):
            SCHED[eng].append(("w", st_out.name, st_out.n))
            SCHED[eng].append(("w", cst.name, cst.n))
            E.ops[eng].append(fin)

        @block.sync
        def _(eng):
            for f in E.ops["sync"]:
                f(eng)

        @block.gpsimd
        def _(eng):
            for f in E.ops["gpsimd"]:
                f(eng)

        @block.tensor
        def _(eng):
            for f in E.ops["tensor"]:
                f(eng)

        @block.vector
        def _(eng):
            for f in E.ops["vector"]:
                f(eng)

        @block.scalar
        def _(eng):
            for f in E.ops["scalar"]:
                f(eng)

    return nc


_CACHE = {}


def _prep_shared(inputs):
    asc = np.ascontiguousarray
    f32 = np.float32

    def colpar(v):
        return asc(v.reshape(NDT, 128).T.astype(f32))

    d = {}
    d["w_inT"] = asc(inputs["L0_attn_w_in"].T.astype(f32))
    d["cwT"] = asc(inputs["L0_attn_conv_w"].transpose(2, 1, 0).astype(f32))
    d["w_outT"] = asc(inputs["L0_attn_w_out"].T.astype(f32))
    d["w_BT"] = asc(inputs["L0_attn_w_B"].T.astype(f32))
    d["w_CT"] = asc(inputs["L0_attn_w_C"].T.astype(f32))
    d["w_d1T"] = asc(inputs["L0_attn_w_d1"].T.astype(f32))
    d["w_d2T"] = asc(inputs["L0_attn_w_d2"].T.astype(f32))
    d["rW"] = asc(inputs["L0_ffn_router_w"].astype(f32))
    d["w1"] = asc(inputs["L0_ffn_w1"].astype(f32))
    d["w3"] = asc(inputs["L0_ffn_w3"].astype(f32))
    d["w2"] = asc(inputs["L0_ffn_w2"].astype(f32))
    d["bin_c"] = colpar(inputs["L0_attn_b_in"])
    d["cb_c"] = colpar(inputs["L0_attn_conv_b"])
    d["bd2_c"] = colpar(inputs["L0_attn_b_d2"])
    d["bout_c"] = colpar(inputs["L0_attn_b_out"])
    d["dpar_c"] = colpar(inputs["L0_attn_D"])
    d["alog_c"] = asc(inputs["L0_attn_A_log"].reshape(NDT, 128, NST)
                      .transpose(1, 0, 2).reshape(128, NDT * NST).astype(f32))
    d["bB"] = asc(inputs["L0_attn_b_B"].reshape(NST, 1).astype(f32))
    d["bC"] = asc(inputs["L0_attn_b_C"].reshape(NST, 1).astype(f32))
    d["bd1"] = asc(inputs["L0_attn_b_d1"].reshape(ND, 1).astype(f32))
    d["rb"] = asc(inputs["L0_ffn_router_b"].reshape(1, NE).astype(f32))
    d["identc"] = np.eye(128, dtype=f32)
    sel16 = np.zeros((NST, NST * 128), dtype=f32)
    for n in range(NST):
        sel16[n, n * 128:(n + 1) * 128] = 1.0
    d["sel16c"] = sel16
    sel4 = np.zeros((NE, NE * 128), dtype=f32)
    for e in range(NE):
        sel4[e, e * 128:(e + 1) * 128] = 1.0
    d["sel4c"] = sel4
    d["onesr"] = np.ones((1, 128), dtype=f32)
    d["onesc"] = np.ones((128, 1), dtype=f32)
    cb = np.zeros((128, 2), dtype=f32)
    cb[:, 0] = EPS
    cb[:, 1] = 1.0
    d["cbias"] = cb
    return d


def kernel(**inputs):
    x = np.asarray(inputs["L0_attn_x"], dtype=np.float32)
    shared = _prep_shared(inputs)
    stage = os.environ.get("KSTAGE", "full")
    if stage not in _CACHE:
        _CACHE[stage] = build_program(stage)
    nc = _CACHE[stage]
    in_maps = []
    for c in range(NCORE):
        m = dict(shared)
        xs = x[BL * c: BL * (c + 1)]
        m["x_fm"] = np.ascontiguousarray(
            xs.reshape(BL, NCH, CH, NDT, 128).transpose(0, 1, 4, 3, 2)
            .reshape(BL, NCH, 128, NDT * CH))
        in_maps.append(m)
    res = run_bass_kernel_spmd(nc, in_maps, core_ids=list(range(NCORE)))
    outs = [r["out_fm"].reshape(BL, NCH, 128, NDT, CH).transpose(0, 1, 4, 3, 2)
            .reshape(BL, L, D) for r in res.results]
    return np.ascontiguousarray(np.concatenate(outs, axis=0))
